# version 10
# baseline (speedup 1.0000x reference)
import contextlib
import numpy as np
import ml_dtypes
import concourse.bass as bass
import concourse.mybir as mybir
from concourse.bass_utils import run_bass_kernel_spmd

F32 = mybir.dt.float32
BF16 = mybir.dt.bfloat16
AF = mybir.ActivationFunctionType
ALU = mybir.AluOpType

B, T, D = 2, 4096, 2048
HQ, HKV, HD = 32, 8, 64
TOK = 1024
INTER = 5632
NIT = INTER // 128
NTC = T // 512
NKT = T // 128
C_SHIFT = 10.0
EPS = 1e-5
N_CORES = 8


class Sems:
    def __init__(self, nc, es):
        self.nc = nc
        self.es = es
        self.cnt = {}
        self.h = {}

    def new(self, *names):
        for n in names:
            self.h[n] = self.es.enter_context(self.nc.semaphore(n))
            self.cnt[n] = 0

    def inc(self, name, n=1):
        self.cnt[name] += n
        return self.h[name], n

    def __getitem__(self, name):
        return self.h[name]


def build_program(debug=None):
    nc = bass.Bass("TRN2", target_bir_lowering=False)

    d_xT = nc.dram_tensor("xT", [D, T], F32, kind="ExternalInput")
    d_xqT = nc.dram_tensor("xqT", [D, TOK], F32, kind="ExternalInput")
    d_xres = nc.dram_tensor("xres", [TOK, D], F32, kind="ExternalInput").ap()
    d_wqT = nc.dram_tensor("wqT", [16 * 128, 2048], F32,
                           kind="ExternalInput").ap()
    d_wkT = nc.dram_tensor("wkT", [D, 512], F32, kind="ExternalInput")
    d_wvT = nc.dram_tensor("wvT", [D, 512], F32, kind="ExternalInput")
    d_woT = nc.dram_tensor("woT", [D, D], F32, kind="ExternalInput").ap()
    d_bqt = nc.dram_tensor("bqt", [128, 16], F32, kind="ExternalInput").ap()
    d_bkt = nc.dram_tensor("bkt", [128, 4], F32, kind="ExternalInput").ap()
    d_bvb = nc.dram_tensor("bvb", [128, 512], F32, kind="ExternalInput").ap()
    d_bob = nc.dram_tensor("bob", [128, D], F32, kind="ExternalInput").ap()
    d_gwT = nc.dram_tensor("gwT", [NIT * 128, 2048], F32,
                           kind="ExternalInput").ap()
    d_uwT = nc.dram_tensor("uwT", [NIT * 128, 2048], F32,
                           kind="ExternalInput").ap()
    d_dwT = nc.dram_tensor("dwT", [INTER, D], F32, kind="ExternalInput").ap()
    d_gst = nc.dram_tensor("gst", [128, NIT], F32, kind="ExternalInput").ap()
    d_ust = nc.dram_tensor("ust", [128, NIT], F32, kind="ExternalInput").ap()
    d_dsb = nc.dram_tensor("dsb", [128, D], F32, kind="ExternalInput").ap()
    d_cosK = nc.dram_tensor("cosK", [128, T], F32, kind="ExternalInput").ap()
    d_sinK = nc.dram_tensor("sinK", [128, T], F32, kind="ExternalInput").ap()
    d_cosQ = nc.dram_tensor("cosQ", [128, TOK], F32, kind="ExternalInput").ap()
    d_sinQ = nc.dram_tensor("sinQ", [128, TOK], F32, kind="ExternalInput").ap()
    d_idb = nc.dram_tensor("identb", [128, 128], BF16, kind="ExternalInput").ap()
    d_out = nc.dram_tensor("out", [TOK, D], F32, kind="ExternalOutput").ap()
    d_zb = nc.dram_tensor("zb", [64, 1024], F32)
    d_ydr = nc.dram_tensor("ydr", [TOK, D], F32)

    dbg_outs = {}

    def dbg(name, shape, dt):
        dbg_outs[name] = nc.dram_tensor(name, shape, dt,
                                        kind="ExternalOutput").ap()
        return dbg_outs[name]

    es = contextlib.ExitStack()
    S = Sems(nc, es)
    S.new("x", "xb", "mmKV", "kev", "vev", "sh", "rope",
          "wld", "wcast", "mmQ", "qev", "qsh", "qrope",
          "psS", "psE", "psA", "zr", "zbd", "bc", "aev",
          "wo_x", "wo_mm", "wo_ev", "ln1", "ysp", "f1c", "f1mm", "f1ev",
          "f1w", "f1wc", "gu_mm", "gu_ev", "hdn",
          "f2w", "f2c", "f2mm", "f2ev", "ln2", "outd", "misc",
          "x0", "x1", "x2", "x3", "wq0", "wq1", "wox0", "wox1", "f1w0", "f1w1",
          "f2w0", "f2w1", "f2w2", "zb0", "zb1", "bc0", "bc1", "ysp0", "ysp1",
          "pre", "tabK", "tabQ", "bobd", "dsbd", "wo0", "wo1",
          "ropev", "qropev", "xq0", "xq1", "xqc", "ropea", "tab0", "tab1")

    with es:
        bvb = es.enter_context(nc.sbuf_tensor("s_bvb", [128, 512], F32))
        bkt = es.enter_context(nc.sbuf_tensor("s_bkt", [128, 4], F32))
        bqt = es.enter_context(nc.sbuf_tensor("s_bqt", [128, 16], F32))
        gst = es.enter_context(nc.sbuf_tensor("s_gst", [128, NIT], F32))
        ust = es.enter_context(nc.sbuf_tensor("s_ust", [128, NIT], F32))
        cneg = es.enter_context(nc.sbuf_tensor("cneg", [128, 1], F32))
        epsp = es.enter_context(nc.sbuf_tensor("epsp", [128, 1], F32))
        ident = es.enter_context(nc.sbuf_tensor("ident", [128, 128], BF16))
        lt = es.enter_context(nc.sbuf_tensor("lt", [128, 8], F32))

        for ap_d, t_s in ((d_bvb, bvb), (d_bkt, bkt), (d_bqt, bqt),
                          (d_gst, gst), (d_ust, ust), (d_idb, ident)):
            nc.scalar.dma_start(t_s[:], ap_d[:]).then_inc(*S.inc("pre", 16))
        nc.vector.memset(cneg[:], -C_SHIFT)
        nc.vector.memset(epsp[:], EPS)

        kv_stack = contextlib.ExitStack()
        xq_right = contextlib.ExitStack()
        xq_bf = xq_right.enter_context(
            nc.sbuf_tensor("xq_bf", [128, 16 * TOK], BF16, side="right"))
        KT = kv_stack.enter_context(nc.sbuf_tensor("KT", [128, 4 * T], BF16))
        Vp = kv_stack.enter_context(
            nc.sbuf_tensor("Vp", [128, HKV * NKT * 65], BF16))
        nc.vector.memset(
            bass.AP(Vp, 64, [[HKV * NKT * 65, 128], [65, HKV * NKT]]), 1.0)

        with contextlib.ExitStack() as pa1:
            cosb = [pa1.enter_context(
                nc.sbuf_tensor(f"cosb{i}", [128, 512], F32)) for i in range(2)]
            sinb = [pa1.enter_context(
                nc.sbuf_tensor(f"sinb{i}", [128, 512], F32)) for i in range(2)]
            wk_bf = pa1.enter_context(nc.sbuf_tensor("wk_bf", [128, 8192], BF16))
            wv_bf = pa1.enter_context(nc.sbuf_tensor("wv_bf", [128, 8192], BF16))
            Kstg = pa1.enter_context(nc.sbuf_tensor("Kstg", [128, 2048], F32))
            Ksh = pa1.enter_context(nc.sbuf_tensor("Ksh", [128, 2048], F32))
            xq_st = [pa1.enter_context(
                nc.sbuf_tensor(f"xq_st{i}", [128, TOK], F32)) for i in range(2)]
            xbf = [pa1.enter_context(nc.sbuf_tensor(f"xbf{i}", [128, 512], F32))
                   for i in range(4)]
            xbb = [pa1.enter_context(nc.sbuf_tensor(f"xbb{i}", [128, 512], BF16))
                   for i in range(4)]
            kt1 = [pa1.enter_context(nc.sbuf_tensor(f"kt1{i}", [128, 512], F32))
                   for i in range(2)]
            kt2 = [pa1.enter_context(nc.sbuf_tensor(f"kt2{i}", [128, 512], F32))
                   for i in range(2)]
            psK = [pa1.enter_context(nc.psum_tensor(f"psK{i}", [128, 512], F32))
                   for i in range(4)]
            psV = [pa1.enter_context(nc.psum_tensor(f"psV{i}", [128, 512], F32))
                   for i in range(4)]


            with nc.sbuf_tensor("wst", [128, 4096], F32) as wst:
                for dsrc, bfdst in ((d_wkT, wk_bf), (d_wvT, wv_bf)):
                    for half in range(2):
                        src = bass.AP(dsrc, half * 8 * 128 * 512,
                                      [[512, 128], [128 * 512, 8], [1, 512]])
                        dst = bass.AP(wst, 0, [[4096, 128], [512, 8], [1, 512]])
                        if S.cnt["wcast"] > 0:
                            nc.sync.wait_ge(S["wcast"], S.cnt["wcast"])
                        h_, _ = S.inc("wld", 16); c_ = S.cnt["wld"]
                        nc.sync.dma_start(dst, src).then_inc(h_, 16)
                        nc.vector.wait_ge(S["wld"], c_)
                        nc.vector.tensor_copy(
                            bfdst[:, half * 4096:(half + 1) * 4096], wst[:]
                        ).then_inc(*S.inc("wcast"))
                nc.all_engine_barrier()

            xb_b1 = S.cnt["xb"]

            def kv_load(k):
                bi = k % 4
                tc_, d_ = divmod(k, 16)
                if k >= 4:
                    nc.sync.wait_ge(S["xb"], xb_b1 + k - 3)
                nc.sync.dma_start(
                    xbf[bi][:],
                    d_xT.ap()[d_ * 128:(d_ + 1) * 128,
                              tc_ * 512:(tc_ + 1) * 512],
                ).then_inc(*S.inc(f"x{bi}", 16))

            def kv_cast(k):
                bi = k % 4
                nc.vector.wait_ge(S[f"x{bi}"], S.cnt[f"x{bi}"])
                if k >= 4:
                    nc.vector.wait_ge(S["mmKV"], k - 3)
                nc.vector.tensor_copy(xbb[bi][:], xbf[bi][:]).then_inc(
                    *S.inc("xb"))

            def kv_mm(k):
                bi = k % 4
                tc_, d_ = divmod(k, 16)
                nc.tensor.wait_ge(S["xb"], xb_b1 + k + 1)
                if tc_ > 0 and d_ == 0:
                    nc.tensor.wait_ge(S["kev"], tc_)
                    nc.tensor.wait_ge(S["vev"], tc_)
                for m in range(4):
                    nc.tensor.matmul(
                        psK[m][:],
                        wk_bf[:, d_ * 512 + m * 128:d_ * 512 + (m + 1) * 128],
                        xbb[bi][:], start=(d_ == 0), stop=(d_ == 15))
                mm = None
                for tt in range(4):
                    mm = nc.tensor.matmul(
                        psV[tt][:], xbb[bi][:, tt * 128:(tt + 1) * 128],
                        wv_bf[:, d_ * 512:(d_ + 1) * 512],
                        start=(d_ == 0), stop=(d_ == 15))
                mm.then_inc(*S.inc("mmKV"))

            def kv_evac_rope(tc_):
                nc.scalar.wait_ge(S["mmKV"], (tc_ + 1) * 16)
                if tc_ == 0:
                    nc.scalar.wait_ge(S["pre"], 96)
                else:
                    nc.scalar.wait_ge(S["ropev"], 4 * tc_)
                    nc.scalar.wait_ge(S["sh"], 64 * tc_)
                act = None
                for m in range(4):
                    act = nc.scalar.activation(
                        Kstg[:, m * 512:(m + 1) * 512], psK[m][:], AF.Identity,
                        bias=bkt[:, m:m + 1], scale=1.0)
                act.then_inc(*S.inc("kev"))
                nc.scalar.wait_ge(S["kev"], tc_ + 1)
                for gg in range(2):
                    for hswap in range(2):
                        so, do_ = (32, 0) if hswap == 0 else (0, 32)
                        nc.scalar.dma_start(
                            Ksh[gg * 64 + do_:gg * 64 + do_ + 32, :],
                            Kstg[gg * 64 + so:gg * 64 + so + 32, :],
                        ).then_inc(*S.inc("sh", 16))
                nc.vector.wait_ge(S["mmKV"], (tc_ + 1) * 16)
                if tc_ == 0:
                    nc.vector.wait_ge(S["pre"], 96)
                tt_last = None
                for tt in range(4):
                    kt_g = tc_ * 4 + tt
                    dst = bass.AP(Vp, kt_g * 65,
                                  [[HKV * NKT * 65, 128], [2080, HKV], [1, 64]])
                    src = bass.AP(psV[tt], 0, [[512, 128], [64, HKV], [1, 64]])
                    tt_last = nc.vector.tensor_tensor(dst, src, bvb[:], ALU.add)
                tt_last.then_inc(*S.inc("vev"))
                nc.vector.wait_ge(S["sh"], 64 * (tc_ + 1))
                nc.vector.wait_ge(S[f"tab{tc_ % 2}"], S.cnt[f"tab{tc_ % 2}"])
                for m in range(4):
                    gm = 4 * tc_ + m
                    if gm >= 2:
                        nc.vector.wait_ge(S["ropea"], gm - 1)
                    nc.vector.tensor_tensor(
                        kt1[m % 2][:], Ksh[:, m * 512:(m + 1) * 512],
                        sinb[tc_ % 2][:], ALU.mult)
                    nc.vector.tensor_tensor(
                        kt2[m % 2][:], Kstg[:, m * 512:(m + 1) * 512],
                        cosb[tc_ % 2][:], ALU.mult
                    ).then_inc(*S.inc("ropev"))
                last = None
                for m in range(4):
                    gm = 4 * tc_ + m
                    nc.gpsimd.wait_ge(S["ropev"], gm + 1)
                    last = nc.gpsimd.tensor_tensor(
                        KT[:, m * T + tc_ * 512:m * T + (tc_ + 1) * 512],
                        kt1[m % 2][:], kt2[m % 2][:], ALU.add
                    ).then_inc(*S.inc("ropea"))

            def tab_load(tc_):
                bi = tc_ % 2
                if tc_ >= 2:
                    nc.gpsimd.wait_ge(S["ropev"], 4 * (tc_ - 1))
                nc.gpsimd.dma_start(cosb[bi][:],
                                    d_cosK[:, tc_ * 512:(tc_ + 1) * 512]
                                    ).then_inc(*S.inc(f"tab{bi}", 16))
                nc.gpsimd.dma_start(sinb[bi][:],
                                    d_sinK[:, tc_ * 512:(tc_ + 1) * 512]
                                    ).then_inc(*S.inc(f"tab{bi}", 16))

            def xq_load(dq):
                bi = dq % 2
                if dq >= 2:
                    nc.gpsimd.wait_ge(S["xqc"], dq - 1)
                nc.gpsimd.dma_start(xq_st[bi][:],
                                    d_xqT.ap()[dq * 128:(dq + 1) * 128, :]
                                    ).then_inc(*S.inc(f"xq{bi}", 16))

            def xq_cast(dq):
                bi = dq % 2
                nc.vector.wait_ge(S[f"xq{bi}"], S.cnt[f"xq{bi}"])
                nc.vector.tensor_copy(xq_bf[:, dq * TOK:(dq + 1) * TOK],
                                      xq_st[bi][:]).then_inc(*S.inc("xqc"))

            tab_load(0)
            tab_load(1)
            for k in range(4):
                kv_load(k)
                kv_cast(k)
            xq_load(0)
            xq_load(1)
            for g in range(NTC * 16):
                tc, d = divmod(g, 16)
                if g + 4 < NTC * 16:
                    kv_load(g + 4)
                    kv_cast(g + 4)
                kv_mm(g)
                if d == 7 and tc < 8:
                    xq_cast(2 * tc)
                    if 2 * tc + 2 < 16:
                        xq_load(2 * tc + 2)
                if d == 15:
                    kv_evac_rope(tc)
                    if tc + 2 < NTC:
                        tab_load(tc + 2)
                    xq_cast(2 * tc + 1)
                    if 2 * tc + 3 < 16:
                        xq_load(2 * tc + 3)
            nc.all_engine_barrier()
            if debug == "pa1":
                o1 = dbg("dbg_KT", [128, 4 * T], BF16)
                o2 = dbg("dbg_Vp", [128, HKV * NKT * 65], BF16)
                nc.sync.dma_start(o1[:], KT[:]).then_inc(*S.inc("outd", 16))
                nc.sync.dma_start(o2[:], Vp[:]).then_inc(*S.inc("outd", 16))
                nc.sync.wait_ge(S["outd"], S.cnt["outd"])
                nc.all_engine_barrier()

        q_stack = contextlib.ExitStack()
        QT2 = q_stack.enter_context(nc.sbuf_tensor("QT2", [128, 16 * TOK], BF16))
        with contextlib.ExitStack() as pa2:
            cosQ = pa2.enter_context(nc.sbuf_tensor("s_cosQ", [128, TOK], F32))
            sinQ = pa2.enter_context(nc.sbuf_tensor("s_sinQ", [128, TOK], F32))
            wqs_st = [pa2.enter_context(
                nc.sbuf_tensor(f"wqs_st{i}", [128, 2048], F32)) for i in range(2)]
            wqs_bf = [pa2.enter_context(
                nc.sbuf_tensor(f"wqs_bf{i}", [128, 2048], BF16)) for i in range(2)]
            Qstg = [pa2.enter_context(
                nc.sbuf_tensor(f"Qstg{i}", [128, TOK], F32)) for i in range(2)]
            Qsh = [pa2.enter_context(
                nc.sbuf_tensor(f"Qsh{i}", [128, TOK], F32)) for i in range(2)]
            qt1 = [pa2.enter_context(nc.sbuf_tensor(f"qt1{i}", [128, TOK], F32))
                   for i in range(2)]
            qt2 = [pa2.enter_context(nc.sbuf_tensor(f"qt2{i}", [128, TOK], F32))
                   for i in range(2)]
            psQ = [pa2.enter_context(nc.psum_tensor(f"psQ{i}", [128, 512], F32))
                   for i in range(8)]

            nc.scalar.dma_start(cosQ[:], d_cosQ[:]).then_inc(*S.inc("tabQ", 16))
            nc.scalar.dma_start(sinQ[:], d_sinQ[:]).then_inc(*S.inc("tabQ", 16))
            wc_base = S.cnt["wcast"]
            mq_base = S.cnt["mmQ"]

            def q_load(j):
                bi = j % 2
                if j >= 2:
                    nc.sync.wait_ge(S["wcast"], wc_base + j - 1)
                nc.sync.dma_start(
                    wqs_st[bi][:], d_wqT[j * 128:(j + 1) * 128, :]
                ).then_inc(*S.inc(f"wq{bi}", 16))

            def q_cast(j):
                bi = j % 2
                nc.vector.wait_ge(S[f"wq{bi}"], S.cnt[f"wq{bi}"])
                if j >= 2:
                    nc.vector.wait_ge(S["mmQ"], mq_base + j - 1)
                nc.vector.tensor_copy(wqs_bf[bi][:], wqs_st[bi][:]
                                      ).then_inc(*S.inc("wcast"))

            def q_comp(j):
                bi = j % 2
                sbi = j % 2
                nc.tensor.wait_ge(S["wcast"], wc_base + j + 1)
                if j == 0:
                    nc.tensor.wait_ge(S["xqc"], 16)
                pA, pB = psQ[(j % 4) * 2], psQ[(j % 4) * 2 + 1]
                if j >= 4:
                    nc.tensor.wait_ge(S["qev"], j - 3)
                mmq = None
                for d in range(16):
                    nc.tensor.matmul(pA[:],
                                     wqs_bf[bi][:, d * 128:(d + 1) * 128],
                                     xq_bf[:, d * TOK:d * TOK + 512],
                                     start=(d == 0), stop=(d == 15))
                    mmq = nc.tensor.matmul(
                        pB[:], wqs_bf[bi][:, d * 128:(d + 1) * 128],
                        xq_bf[:, d * TOK + 512:(d + 1) * TOK],
                        start=(d == 0), stop=(d == 15))
                mmq.then_inc(*S.inc("mmQ"))
                nc.scalar.wait_ge(S["mmQ"], mq_base + j + 1)
                if j == 0:
                    nc.scalar.wait_ge(S["pre"], 96)
                if j >= 2:
                    nc.scalar.wait_ge(S["qrope"], j - 1)
                nc.scalar.activation(Qstg[sbi][:, 0:512], pA[:], AF.Identity,
                                     bias=bqt[:, j:j + 1], scale=1.0)
                nc.scalar.activation(Qstg[sbi][:, 512:1024], pB[:],
                                     AF.Identity, bias=bqt[:, j:j + 1],
                                     scale=1.0).then_inc(*S.inc("qev"))
                nc.scalar.wait_ge(S["qev"], j + 1)
                if j >= 1:
                    nc.scalar.wait_ge(S["qrope"], j)
                for gg in range(2):
                    for hswap in range(2):
                        so, do_ = (32, 0) if hswap == 0 else (0, 32)
                        nc.scalar.dma_start(
                            Qsh[sbi][gg * 64 + do_:gg * 64 + do_ + 32, :],
                            Qstg[sbi][gg * 64 + so:gg * 64 + so + 32, :],
                        ).then_inc(*S.inc("qsh", 16))
                nc.vector.wait_ge(S["qsh"], S.cnt["qsh"])
                if j == 0:
                    nc.vector.wait_ge(S["tabQ"], 32)
                if j >= 2:
                    nc.vector.wait_ge(S["qrope"], j - 1)
                nc.vector.tensor_tensor(qt1[sbi][:], Qsh[sbi][:], sinQ[:],
                                        ALU.mult)
                nc.vector.tensor_tensor(qt2[sbi][:], Qstg[sbi][:], cosQ[:],
                                        ALU.mult).then_inc(*S.inc("qropev"))
                nc.gpsimd.wait_ge(S["qropev"], j + 1)
                last = None
                for H, rsrc in ((2 * j, 0), (2 * j + 1, 64)):
                    blk = 4 * (H // 8) + (H % 4)
                    r0 = 0 if (H % 8) < 4 else 64
                    last = nc.gpsimd.tensor_tensor(
                        QT2[r0:r0 + 64, blk * TOK:(blk + 1) * TOK],
                        qt1[sbi][rsrc:rsrc + 64, :], qt2[sbi][rsrc:rsrc + 64, :],
                        ALU.add)
                last.then_inc(*S.inc("qrope"))

            q_load(0)
            q_cast(0)
            q_load(1)
            q_cast(1)
            for j in range(16):
                if j + 2 < 16:
                    q_load(j + 2)
                    q_cast(j + 2)
                q_comp(j)
            nc.all_engine_barrier()
            if debug == "pa2":
                o1 = dbg("dbg_QT2", [128, 16 * TOK], BF16)
                nc.sync.dma_start(o1[:], QT2[:]).then_inc(*S.inc("outd", 16))
                nc.sync.wait_ge(S["outd"], S.cnt["outd"])
                nc.all_engine_barrier()

        xq_right.close()
        attn_right = contextlib.ExitStack()
        attnT = attn_right.enter_context(
            nc.sbuf_tensor("attnT", [128, 16 * TOK], BF16, side="right"))
        woA = attn_right.enter_context(
            nc.sbuf_tensor("woA", [128, 10 * D], BF16, side="right"))
        with contextlib.ExitStack() as psx:
            wo_st = [psx.enter_context(
                nc.sbuf_tensor(f"wo_st{i}", [128, 1024], F32)) for i in range(2)]
            Sps = [psx.enter_context(nc.psum_tensor(f"Sps{i}", [128, 1024], F32))
                   for i in range(2)]
            Aps = [psx.enter_context(nc.psum_tensor(f"Aps{i}", [65, 512], F32))
                   for i in range(4)]
            wT = [psx.enter_context(nc.sbuf_tensor(f"wT{i}", [128, 1024], BF16))
                  for i in range(4)]
            zrec = [psx.enter_context(nc.sbuf_tensor(f"zrec{i}", [1, 1024], F32))
                    for i in range(2)]
            bcs = [psx.enter_context(nc.sbuf_tensor(f"bcs{i}", [128, 1024], F32))
                   for i in range(2)]

            seq = [(m, i, qb) for m in range(4) for i in range(4)
                   for qb in range(2)]

            def ps_S(itx, kt):
                m, i, qb = seq[itx]
                blk = 4 * m + i
                sbi = kt % 2
                wbi = kt % 4
                g = itx * NKT + kt
                if g >= 2:
                    nc.tensor.wait_ge(S["psE"], g - 1)
                nc.tensor.matmul(
                    Sps[sbi][:, 0:512],
                    KT[0:64, m * T + kt * 128:m * T + (kt + 1) * 128],
                    QT2[0:64, blk * TOK + qb * 512:blk * TOK + (qb + 1) * 512],
                    start=True, stop=True, tile_position=(0, 0))
                nc.tensor.matmul(
                    Sps[sbi][:, 512:1024],
                    KT[64:128, m * T + kt * 128:m * T + (kt + 1) * 128],
                    QT2[64:128, blk * TOK + qb * 512:blk * TOK + (qb + 1) * 512],
                    start=True, stop=True, tile_position=(64, 0)
                ).then_inc(*S.inc("psS"))
                nc.scalar.wait_ge(S["psS"], g + 1)
                if g >= 4:
                    nc.scalar.wait_ge(S["psA"], g - 3)
                nc.scalar.activation(
                    wT[wbi][:], Sps[sbi][:], AF.Exp,
                    bias=cneg[:], scale=0.125).then_inc(*S.inc("psE"))

            def ps_AV(itx, kt):
                m, i, qb = seq[itx]
                kve, kvo = 2 * m, 2 * m + 1
                aset = itx % 2
                A0, A1 = Aps[aset * 2], Aps[aset * 2 + 1]
                wbi = kt % 4
                g = itx * NKT + kt
                if kt == 0 and itx >= 2:
                    nc.tensor.wait_ge(S["aev"], itx - 1)
                nc.tensor.wait_ge(S["psE"], g + 1)
                nc.tensor.matmul(
                    A0[:],
                    Vp[:, kve * 2080 + kt * 65:kve * 2080 + kt * 65 + 65],
                    wT[wbi][:, 0:512],
                    start=(kt == 0), stop=(kt == NKT - 1))
                nc.tensor.matmul(
                    A1[:],
                    Vp[:, kvo * 2080 + kt * 65:kvo * 2080 + kt * 65 + 65],
                    wT[wbi][:, 512:1024],
                    start=(kt == 0), stop=(kt == NKT - 1)
                ).then_inc(*S.inc("psA"))

            def ps_epilogue(itx):
                m, i, qb = seq[itx]
                He, Ho = 8 * m + i, 8 * m + 4 + i
                aset = itx % 2
                A0, A1 = Aps[aset * 2], Aps[aset * 2 + 1]
                bset = itx % 2
                nc.vector.wait_ge(S["psA"], (itx + 1) * NKT)
                if itx >= 2:
                    nc.vector.wait_ge(S[f"zb{bset}"], S.cnt[f"zb{bset}"])
                nc.vector.reciprocal(zrec[bset][0:1, 0:512], A0[64:65, :])
                nc.vector.reciprocal(zrec[bset][0:1, 512:1024],
                                     A1[64:65, :]).then_inc(*S.inc("zr"))
                nc.gpsimd.wait_ge(S["zr"], S.cnt["zr"])
                h_, _ = S.inc(f"zb{bset}", 16); c_ = S.cnt[f"zb{bset}"]
                nc.gpsimd.dma_start(
                    bass.AP(d_zb, itx * 1024, [[1024, 1], [1, 1024]]),
                    zrec[bset][0:1, :]).then_inc(h_, 16)
                nc.gpsimd.wait_ge(S[f"zb{bset}"], c_)
                if itx >= 2:
                    nc.gpsimd.wait_ge(S["aev"], itx - 1)
                h_, _ = S.inc(f"bc{bset}", 16); c_ = S.cnt[f"bc{bset}"]
                nc.gpsimd.dma_start(
                    bcs[bset][:],
                    bass.AP(d_zb, itx * 1024, [[0, 128], [1, 1024]]),
                ).then_inc(h_, 16)
                nc.vector.wait_ge(S[f"bc{bset}"], c_)
                lastev = None
                for H, A_, co in ((He, A0, 0), (Ho, A1, 512)):
                    bj = H // 2
                    r0 = (H % 2) * 64
                    lastev = nc.vector.tensor_tensor(
                        attnT[r0:r0 + 64,
                              bj * TOK + qb * 512:bj * TOK + (qb + 1) * 512],
                        A_[0:64, :], bcs[bset][0:64, co:co + 512], ALU.mult)
                lastev.then_inc(*S.inc("aev"))

            woc_base = S.cnt["wcast"]

            def wo_preload(k):
                bi = k % 2
                j, half = divmod(k, 2)
                if k >= 2:
                    nc.sync.wait_ge(S["wcast"], woc_base + k - 1)
                nc.sync.dma_start(
                    wo_st[bi][:],
                    d_woT[j * 128:(j + 1) * 128,
                          half * 1024:(half + 1) * 1024]
                ).then_inc(*S.inc(f"wo{bi}", 16))
                nc.vector.wait_ge(S[f"wo{bi}"], S.cnt[f"wo{bi}"])
                nc.vector.tensor_copy(
                    woA[:, j * D + half * 1024:j * D + (half + 1) * 1024],
                    wo_st[bi][:]).then_inc(*S.inc("wcast"))

            GPS = len(seq) * NKT
            ps_S(0, 0)
            ps_S(0, 1)
            for g in range(GPS):
                itx, kt = divmod(g, NKT)
                ps_AV(itx, kt)
                if g + 2 < GPS:
                    ps_S((g + 2) // NKT, (g + 2) % NKT)
                if kt == NKT - 1:
                    ps_epilogue(itx)
                    if itx < 10:
                        wo_preload(2 * itx)
                        wo_preload(2 * itx + 1)
            nc.all_engine_barrier()
            if debug == "ps":
                o1 = dbg("dbg_attnT", [128, 16 * TOK], BF16)
                nc.sync.dma_start(o1[:], attnT[:]).then_inc(*S.inc("outd", 16))
                nc.sync.wait_ge(S["outd"], S.cnt["outd"])
                nc.all_engine_barrier()

        q_stack.close()
        kv_stack.close()

        yt_stack = contextlib.ExitStack()
        yT = yt_stack.enter_context(nc.sbuf_tensor("yT", [128, 16 * TOK], BF16))
        with contextlib.ExitStack() as wo:
            woB = wo.enter_context(nc.sbuf_tensor("woB", [128, 6 * D], BF16))
            with contextlib.ExitStack() as pre:
                wo_st2 = [pre.enter_context(
                    nc.sbuf_tensor(f"wo_st2{i}", [128, 1024], F32))
                    for i in range(2)]
                wc_base = S.cnt["wcast"]
                for k in range(12):
                    bi = k % 2
                    j, half = divmod(k, 2)
                    if k >= 2:
                        nc.sync.wait_ge(S["wcast"], wc_base + k - 1)
                    nc.sync.dma_start(
                        wo_st2[bi][:],
                        d_woT[(10 + j) * 128:(11 + j) * 128,
                              half * 1024:(half + 1) * 1024]
                    ).then_inc(*S.inc(f"wo{bi}", 16))
                    nc.vector.wait_ge(S[f"wo{bi}"], S.cnt[f"wo{bi}"])
                    nc.vector.tensor_copy(
                        woB[:, j * D + half * 1024:j * D + (half + 1) * 1024],
                        wo_st2[bi][:]).then_inc(*S.inc("wcast"))
                nc.vector.drain().then_inc(*S.inc("wcast"))
                woB_target = S.cnt["wcast"]
            bob = wo.enter_context(nc.sbuf_tensor("s_bob", [128, D], F32))
            xr = [wo.enter_context(nc.sbuf_tensor(f"xr{i}", [128, D], F32))
                  for i in range(2)]
            wtmp = [wo.enter_context(nc.sbuf_tensor(f"wtmp{i}", [128, 512], F32))
                    for i in range(4)]
            ytile = [wo.enter_context(nc.sbuf_tensor(f"ytile{i}", [128, D], F32))
                     for i in range(2)]
            ybf = [wo.enter_context(nc.sbuf_tensor(f"ybf{i}", [128, D], BF16))
                   for i in range(2)]
            junk = wo.enter_context(nc.sbuf_tensor("junk", [128, D], F32))
            psW = [wo.enter_context(nc.psum_tensor(f"psW{i}", [128, 512], F32))
                   for i in range(8)]

            nc.scalar.wait_ge(S["wcast"], woB_target)
            nc.scalar.dma_start(bob[:], d_bob[:]).then_inc(*S.inc("bobd", 16))
            nc.scalar.wait_ge(S["bobd"], 16)
            nc.all_engine_barrier()

            def emit_transposes(qt):
                par = qt % 2
                ws = par * 4
                if qt == 0:
                    nc.tensor.wait_ge(S["pre"], 96)
                nc.tensor.wait_ge(S["f1c"], qt + 1)
                lastt = None
                for k in range(16):
                    kk = k % 8
                    pv = bass.AP(psW[ws + k // 8], kk * 64,
                                 [[512, 128], [1, 64]]).bitcast(BF16)
                    lastt = nc.tensor.transpose(
                        pv, ybf[par][:, k * 128:(k + 1) * 128], ident[:])
                lastt.then_inc(*S.inc("f1mm"))
                nc.vector.wait_ge(S["f1mm"], S.cnt["f1mm"])
                for half in range(2):
                    pv = bass.AP(psW[ws + half], 0,
                                 [[512, 128], [1, 512]]).bitcast(BF16)
                    dst = bass.AP(yT, (half * 8) * TOK + qt * 128,
                                  [[16 * TOK, 128], [TOK, 8], [1, 128]])
                    ev = nc.vector.tensor_copy(dst, pv)
                ev.then_inc(*S.inc("f1ev"))

            for qt in range(8):
                par = qt % 2
                ws = par * 4
                if qt >= 2:
                    nc.sync.wait_ge(S["wo_ev"], qt - 1)
                h_, _ = S.inc(f"wox{par}", 16); c_ = S.cnt[f"wox{par}"]
                nc.sync.dma_start(xr[par][:],
                                  d_xres[qt * 128:(qt + 1) * 128, :]
                                  ).then_inc(h_, 16)
                if qt >= 2:
                    nc.tensor.wait_ge(S["wo_ev"], qt - 1)
                    nc.tensor.wait_ge(S["f1ev"], qt - 1)
                mmwo = None
                for j in range(16):
                    if qt == 0 and j >= 10:
                        nc.tensor.wait_ge(S["wcast"],
                                          woB_target - (15 - j) * 2)
                    wsrc = (woA[:, j * D:(j + 1) * D] if j < 10
                            else woB[:, (j - 10) * D:(j - 9) * D])
                    for ds in range(4):
                        mmwo = nc.tensor.matmul(
                            psW[ws + ds][:],
                            attnT[:, j * TOK + qt * 128:j * TOK + (qt + 1) * 128],
                            wsrc[:, ds * 512:(ds + 1) * 512],
                            start=(j == 0), stop=(j == 15))
                mmwo.then_inc(*S.inc("wo_mm"))
                nc.vector.wait_ge(S["wo_mm"], qt + 1)
                nc.vector.wait_ge(S[f"wox{par}"], c_)
                if qt >= 2:
                    nc.vector.wait_ge(S[f"ysp{par}"],
                                      S.cnt[f"ysp{par}"])
                for ds in range(4):
                    nc.vector.tensor_tensor(
                        wtmp[ds][:], psW[ws + ds][:],
                        xr[par][:, ds * 512:(ds + 1) * 512], ALU.add)
                nc.vector.drain()
                lastv = None
                for ds in range(4):
                    lastv = nc.vector.tensor_tensor(
                        ytile[par][:, ds * 512:(ds + 1) * 512],
                        wtmp[ds][:], bob[:, ds * 512:(ds + 1) * 512], ALU.add)
                lastv.then_inc(*S.inc("wo_ev"))
                rseg = ytile[par][:]
                nc.scalar.wait_ge(S["wo_ev"], qt + 1)
                nc.scalar.activation(junk[:], rseg, AF.Square,
                                     accum_out=lt[:, 1:2])
                nc.scalar.drain()
                nc.scalar.activation(junk[:], rseg, AF.Copy,
                                     accum_out=lt[:, 0:1])
                nc.scalar.drain()
                nc.scalar.activation(lt[:, 2:3], lt[:, 0:1], AF.Copy,
                                     scale=-1.0 / D)
                nc.scalar.drain()
                nc.scalar.activation(lt[:, 3:4], lt[:, 2:3], AF.Square)
                nc.scalar.drain()
                nc.scalar.activation(lt[:, 4:5], lt[:, 1:2], AF.Copy,
                                     scale=1.0 / D).then_inc(*S.inc("ln1"))
                nc.vector.wait_ge(S["ln1"], S.cnt["ln1"])
                nc.vector.tensor_tensor(lt[:, 5:6], lt[:, 4:5], lt[:, 3:4],
                                        ALU.subtract)
                nc.vector.drain()
                nc.vector.tensor_tensor(lt[:, 6:7], lt[:, 5:6], epsp[:],
                                        ALU.add).then_inc(*S.inc("ln1"))
                nc.scalar.wait_ge(S["ln1"], S.cnt["ln1"])
                nc.scalar.activation(lt[:, 7:8], lt[:, 6:7], AF.Ln)
                nc.scalar.drain()
                nc.scalar.activation(lt[:, 6:7], lt[:, 7:8], AF.Exp,
                                     scale=-0.5).then_inc(*S.inc("ln1"))
                nc.vector.wait_ge(S["ln1"], S.cnt["ln1"])
                nc.vector.tensor_scalar_mul(lt[:, 7:8], lt[:, 2:3],
                                            lt[:, 6:7]).then_inc(*S.inc("ln1"))
                nc.scalar.wait_ge(S["ln1"], S.cnt["ln1"])
                nc.scalar.activation(rseg, rseg, AF.Identity, bias=lt[:, 7:8],
                                     scale=lt[:, 6:7]).then_inc(*S.inc("ln1"))
                nc.sync.wait_ge(S["ln1"], S.cnt["ln1"])
                nc.sync.dma_start(d_ydr.ap()[qt * 128:(qt + 1) * 128, :], rseg
                                  ).then_inc(*S.inc(f"ysp{par}", 16))
                nc.vector.wait_ge(S["ln1"], S.cnt["ln1"])
                if qt >= 2:
                    nc.vector.wait_ge(S["f1mm"], qt - 1)
                nc.vector.tensor_copy(ybf[par][:], rseg).then_inc(*S.inc("f1c"))
                emit_transposes(qt)
            nc.sync.wait_ge(S["ysp0"], S.cnt["ysp0"])
            nc.sync.wait_ge(S["ysp1"], S.cnt["ysp1"])
            nc.all_engine_barrier()
            if debug == "wo":
                o1 = dbg("dbg_yT", [128, 16 * TOK], BF16)
                nc.sync.dma_start(o1[:], yT[:]).then_inc(*S.inc("outd", 16))
                nc.sync.wait_ge(S["outd"], S.cnt["outd"])
                nc.all_engine_barrier()

        attn_right.close()
        f1_right = contextlib.ExitStack()
        hdnT = f1_right.enter_context(
            nc.sbuf_tensor("hdnT", [128, NIT * TOK], BF16, side="right"))
        with contextlib.ExitStack() as f1:
            g_st = [f1.enter_context(nc.sbuf_tensor(f"g_st{i}", [128, D], F32))
                    for i in range(2)]
            g_bf = [f1.enter_context(nc.sbuf_tensor(f"g_bf{i}", [128, D], BF16))
                    for i in range(2)]
            u_st = [f1.enter_context(nc.sbuf_tensor(f"u_st{i}", [128, D], F32))
                    for i in range(2)]
            u_bf = [f1.enter_context(nc.sbuf_tensor(f"u_bf{i}", [128, D], BF16))
                    for i in range(2)]
            sg = [f1.enter_context(nc.sbuf_tensor(f"sg{i}", [128, 1024], BF16))
                  for i in range(2)]
            su = [f1.enter_context(nc.sbuf_tensor(f"su{i}", [128, 1024], BF16))
                  for i in range(2)]
            psG = [f1.enter_context(nc.psum_tensor(f"psG{i}", [128, 1024], F32))
                   for i in range(4)]

            f1w_base = S.cnt["f1wc"]
            gu_base = S.cnt["gu_mm"]
            hdn_base = S.cnt["hdn"]
            ev_base = S.cnt["gu_ev"]

            def f1_load(k):
                bi = k % 2
                if k >= 2:
                    nc.sync.wait_ge(S["f1wc"], f1w_base + k - 1)
                for dsrc, st_ in ((d_gwT, g_st[bi]), (d_uwT, u_st[bi])):
                    nc.sync.dma_start(st_[:],
                                      dsrc[k * 128:(k + 1) * 128, :]
                                      ).then_inc(*S.inc(f"f1w{bi}", 16))

            def f1_cast(k):
                bi = k % 2
                nc.vector.wait_ge(S[f"f1w{bi}"], S.cnt[f"f1w{bi}"])
                if k >= 2:
                    nc.vector.wait_ge(S["gu_mm"], gu_base + k - 1)
                nc.vector.tensor_copy(g_bf[bi][:], g_st[bi][:])
                nc.vector.tensor_copy(u_bf[bi][:], u_st[bi][:]
                                      ).then_inc(*S.inc("f1wc"))

            def f1_comp(k):
                bi = k % 2
                pg, pu = psG[bi * 2], psG[bi * 2 + 1]
                nc.tensor.wait_ge(S["f1wc"], f1w_base + k + 1)
                if k >= 2:
                    nc.tensor.wait_ge(S["hdn"], hdn_base + k - 1)
                lastm = None
                for hc in range(16):
                    for pp, wbf in ((pg, g_bf[bi]), (pu, u_bf[bi])):
                        for th in range(2):
                            lastm = nc.tensor.matmul(
                                pp[:, th * 512:(th + 1) * 512],
                                wbf[:, hc * 128:(hc + 1) * 128],
                                yT[:, hc * TOK + th * 512:
                                   hc * TOK + (th + 1) * 512],
                                start=(hc == 0), stop=(hc == 15))
                lastm.then_inc(*S.inc("gu_mm"))
                nc.scalar.wait_ge(S["gu_mm"], gu_base + k + 1)
                if k == 0:
                    nc.scalar.wait_ge(S["pre"], 96)
                if k >= 2:
                    nc.scalar.wait_ge(S["hdn"], hdn_base + k - 1)
                nc.scalar.activation(sg[bi][:], pg[:], AF.Silu,
                                     scale=gst[:, k:k + 1])
                nc.scalar.activation(su[bi][:], pu[:], AF.Copy,
                                     scale=ust[:, k:k + 1]
                                     ).then_inc(*S.inc("gu_ev"))
                nc.gpsimd.wait_ge(S["gu_ev"], ev_base + k + 1)
                nc.gpsimd.tensor_tensor(hdnT[:, k * TOK:(k + 1) * TOK],
                                        sg[bi][:], su[bi][:], ALU.mult
                                        ).then_inc(*S.inc("hdn"))

            f1_load(0)
            f1_cast(0)
            f1_load(1)
            f1_cast(1)
            for k in range(NIT):
                if k + 2 < NIT:
                    f1_load(k + 2)
                    f1_cast(k + 2)
                f1_comp(k)
            nc.all_engine_barrier()
            if debug == "ffn1":
                o1 = dbg("dbg_hdnT", [128, NIT * TOK], BF16)
                nc.sync.dma_start(o1[:], hdnT[:]).then_inc(*S.inc("outd", 16))
                nc.sync.wait_ge(S["outd"], S.cnt["outd"])
                nc.all_engine_barrier()

        yt_stack.close()
        with contextlib.ExitStack() as f2:
            res2 = f2.enter_context(nc.sbuf_tensor("res2", [128, 8 * D], F32))
            dsb = f2.enter_context(nc.sbuf_tensor("s_dsb", [128, D], F32))
            lt2 = f2.enter_context(nc.sbuf_tensor("lt2", [128, 64], F32))
            dw_st = [f2.enter_context(
                nc.sbuf_tensor(f"dw_st{i}", [128, 512], F32)) for i in range(3)]
            dw_bf = [f2.enter_context(
                nc.sbuf_tensor(f"dw_bf{i}", [128, 512], BF16)) for i in range(3)]
            dtmp = [f2.enter_context(
                nc.sbuf_tensor(f"dtmp{i}", [128, 512], F32)) for i in range(8)]
            psD = [f2.enter_context(nc.psum_tensor(f"psD{i}", [128, 512], F32))
                   for i in range(8)]

            nc.scalar.wait_ge(S["ysp0"], S.cnt["ysp0"])
            nc.scalar.wait_ge(S["ysp1"], S.cnt["ysp1"])
            nc.scalar.dma_start(dsb[:], d_dsb[:]).then_inc(*S.inc("dsbd", 16))
            for qt in range(8):
                nc.scalar.dma_start(res2[:, qt * D:(qt + 1) * D],
                                    d_ydr.ap()[qt * 128:(qt + 1) * 128, :]
                                    ).then_inc(*S.inc("dsbd", 16))
            nc.scalar.wait_ge(S["dsbd"], 144)
            nc.all_engine_barrier()

            f2c_base = S.cnt["f2c"]
            f2m_base = S.cnt["f2mm"]

            def f2_load(g):
                bi = g % 3
                hs_, k_ = divmod(g, NIT)
                if g >= 3:
                    nc.sync.wait_ge(S["f2c"], f2c_base + g - 2)
                nc.sync.dma_start(
                    dw_st[bi][:],
                    d_dwT[k_ * 128:(k_ + 1) * 128,
                          hs_ * 512:(hs_ + 1) * 512]
                ).then_inc(*S.inc(f"f2w{bi}", 16))

            def f2_cast(g):
                bi = g % 3
                nc.vector.wait_ge(S[f"f2w{bi}"], S.cnt[f"f2w{bi}"])
                if g >= 3:
                    nc.vector.wait_ge(S["f2mm"], f2m_base + g - 2)
                nc.vector.tensor_copy(dw_bf[bi][:], dw_st[bi][:]
                                      ).then_inc(*S.inc("f2c"))

            def f2_comp(g):
                bi = g % 3
                hs_, k_ = divmod(g, NIT)
                nc.tensor.wait_ge(S["f2c"], f2c_base + g + 1)
                if k_ == 0 and hs_ > 0:
                    nc.tensor.wait_ge(S["f2ev"], hs_)
                lastm = None
                for tt in range(8):
                    lastm = nc.tensor.matmul(
                        psD[tt][:],
                        hdnT[:, k_ * TOK + tt * 128:k_ * TOK + (tt + 1) * 128],
                        dw_bf[bi][:],
                        start=(k_ == 0), stop=(k_ == NIT - 1))
                lastm.then_inc(*S.inc("f2mm"))

            def f2_evac(hs_):
                nc.vector.wait_ge(S["f2mm"], f2m_base + (hs_ + 1) * NIT)
                for tt in range(8):
                    nc.vector.tensor_tensor(dtmp[tt][:], psD[tt][:],
                                            dsb[:, hs_ * 512:(hs_ + 1) * 512],
                                            ALU.mult)
                nc.vector.drain()
                laste = None
                for tt in range(8):
                    seg = res2[:, tt * D + hs_ * 512:tt * D + (hs_ + 1) * 512]
                    laste = nc.vector.tensor_tensor(seg, seg, dtmp[tt][:],
                                                    ALU.add)
                laste.then_inc(*S.inc("f2ev"))

            NG = 4 * NIT
            for g0 in range(3):
                f2_load(g0)
                f2_cast(g0)
            for g in range(NG):
                if g + 3 < NG:
                    f2_load(g + 3)
                    f2_cast(g + 3)
                f2_comp(g)
                if g % NIT == NIT - 1:
                    f2_evac(g // NIT)
            nc.all_engine_barrier()

            qstat = None
            for qt in range(8):
                rseg = res2[:, qt * D:(qt + 1) * D]
                qstat = nc.scalar.activation(dsb[:], rseg, AF.Square,
                                             accum_out=lt2[:, 8 + qt:9 + qt])
                nc.vector.tensor_reduce(lt2[:, qt:qt + 1], rseg,
                                        mybir.AxisListType.X, ALU.add)
            qstat.then_inc(*S.inc("ln2"))
            nc.vector.drain()
            nc.vector.wait_ge(S["ln2"], S.cnt["ln2"])
            nc.vector.tensor_scalar_mul(lt2[:, 16:24], lt2[:, 0:8], -1.0 / D)
            nc.vector.tensor_scalar_mul(lt2[:, 24:32], lt2[:, 8:16], 1.0 / D)
            nc.vector.drain()
            nc.vector.tensor_tensor(lt2[:, 32:40], lt2[:, 16:24],
                                    lt2[:, 16:24], ALU.mult)
            nc.vector.drain()
            nc.vector.tensor_tensor(lt2[:, 40:48], lt2[:, 24:32],
                                    lt2[:, 32:40], ALU.subtract)
            nc.vector.drain()
            nc.vector.tensor_scalar_add(lt2[:, 48:56], lt2[:, 40:48], EPS
                                        ).then_inc(*S.inc("ln2"))
            nc.scalar.wait_ge(S["ln2"], S.cnt["ln2"])
            nc.scalar.activation(lt2[:, 40:48], lt2[:, 48:56], AF.Ln)
            nc.scalar.drain()
            nc.scalar.activation(lt2[:, 48:56], lt2[:, 40:48], AF.Exp,
                                 scale=-0.5).then_inc(*S.inc("ln2"))
            nc.vector.wait_ge(S["ln2"], S.cnt["ln2"])
            nc.vector.tensor_tensor(lt2[:, 56:64], lt2[:, 16:24],
                                    lt2[:, 48:56], ALU.mult
                                    ).then_inc(*S.inc("ln2"))
            nc.scalar.wait_ge(S["ln2"], S.cnt["ln2"])
            for qt in range(8):
                rseg = res2[:, qt * D:(qt + 1) * D]
                nc.scalar.activation(rseg, rseg, AF.Identity,
                                     bias=lt2[:, 56 + qt:57 + qt],
                                     scale=lt2[:, 48 + qt:49 + qt]
                                     ).then_inc(*S.inc("ln2"))
                nc.sync.wait_ge(S["ln2"], S.cnt["ln2"])
                nc.sync.dma_start(d_out[qt * 128:(qt + 1) * 128, :], rseg
                                  ).then_inc(*S.inc("outd", 16))
            nc.sync.wait_ge(S["outd"], S.cnt["outd"])

        f1_right.close()

    return nc, sorted(dbg_outs)


def _rope_tables(t0, n):
    inv = 10000.0 ** (-(np.arange(32, dtype=np.float64) * 2) / 64.0)
    p = np.arange(128)
    t = np.arange(t0, t0 + n, dtype=np.float64)
    ang = t[None, :] * inv[p % 32][:, None]
    cos = np.cos(ang).astype(np.float32)
    sin = np.sin(ang).astype(np.float32)
    sgn = np.where((p % 64) < 32, -1.0, 1.0).astype(np.float32)
    return cos, np.ascontiguousarray(sin * sgn[:, None]).astype(np.float32)


def _prep_inputs(inp):
    c = np.ascontiguousarray

    def _blk(wT, nk):
        a = wT.reshape(2, 8, 128, nk, 128)
        return c(a.transpose(3, 2, 0, 1, 4).reshape(nk * 128, 2048))

    x = np.asarray(inp["L0_attn_x"], dtype=np.float32)
    cosK, sinK = _rope_tables(0, T)
    base = {
        "wqT": _blk(np.asarray(inp["L0_attn_Wq"]).T, 16),
        "wkT": c(np.asarray(inp["L0_attn_Wk"]).T),
        "wvT": c(np.asarray(inp["L0_attn_Wv"]).T),
        "woT": c(np.asarray(inp["L0_attn_Wo"]).T),
        "bqt": c(np.asarray(inp["L0_attn_bq"]).reshape(16, 128).T),
        "bkt": c(np.asarray(inp["L0_attn_bk"]).reshape(4, 128).T),
        "bvb": c(np.broadcast_to(np.asarray(inp["L0_attn_bv"])[None, :],
                                 (128, 512))),
        "bob": c(np.broadcast_to(np.asarray(inp["L0_attn_bo"])[None, :],
                                 (128, D))),
        "gwT": _blk(np.asarray(inp["L0_ffn_gate_w"]).T, NIT),
        "uwT": _blk(np.asarray(inp["L0_ffn_up_w"]).T, NIT),
        "dwT": c(np.asarray(inp["L0_ffn_down_w"]).T),
        "gst": c(np.asarray(inp["L0_ffn_gate_s"]).reshape(NIT, 128).T),
        "ust": c(np.asarray(inp["L0_ffn_up_s"]).reshape(NIT, 128).T),
        "dsb": c(np.broadcast_to(np.asarray(inp["L0_ffn_down_s"])[None, :],
                                 (128, D))),
        "cosK": cosK, "sinK": sinK,
        "identb": np.eye(128).astype(ml_dtypes.bfloat16),
    }
    xT = [c(x[b].T) for b in range(B)]
    maps = []
    for core in range(N_CORES):
        b = core // 4
        q0 = (core % 4) * TOK
        cosQ, sinQ = _rope_tables(q0, TOK)
        m = dict(base)
        m["xT"] = xT[b]
        m["xqT"] = c(xT[b][:, q0:q0 + TOK])
        m["xres"] = c(x[b, q0:q0 + TOK])
        m["cosQ"] = cosQ
        m["sinQ"] = sinQ
        maps.append(m)
    return maps


def kernel(**inputs):
    try:
        from concourse.compiler_utils import (get_compiler_flags,
                                              set_compiler_flags)
        set_compiler_flags([
            f.replace("--enable-ldw-opt=false", "--enable-ldw-opt=true")
            for f in get_compiler_flags()])
    except Exception:
        pass
    maps = _prep_inputs(inputs)
    nc, _ = build_program()
    res = run_bass_kernel_spmd(nc, maps, core_ids=list(range(N_CORES)))
    out = np.empty((B, T, D), np.float32)
    for core in range(N_CORES):
        b = core // 4
        q0 = (core % 4) * TOK
        out[b, q0:q0 + TOK] = np.asarray(res.results[core]["out"])
    return out


# version 11
# speedup vs baseline: 1.1815x; 1.1815x over previous
import contextlib
import numpy as np
import ml_dtypes
import concourse.bass as bass
import concourse.mybir as mybir
from concourse.bass_utils import run_bass_kernel_spmd

F32 = mybir.dt.float32
BF16 = mybir.dt.bfloat16
AF = mybir.ActivationFunctionType
ALU = mybir.AluOpType

B, T, D = 2, 4096, 2048
HQ, HKV, HD = 32, 8, 64
TOK = 1024
INTER = 5632
NIT = INTER // 128
NTC = T // 512
NKT = T // 128
C_SHIFT = 10.0
EPS = 1e-5
N_CORES = 8


class Sems:
    def __init__(self, nc, es):
        self.nc = nc
        self.es = es
        self.cnt = {}
        self.h = {}

    def new(self, *names):
        for n in names:
            self.h[n] = self.es.enter_context(self.nc.semaphore(n))
            self.cnt[n] = 0

    def inc(self, name, n=1):
        self.cnt[name] += n
        return self.h[name], n

    def __getitem__(self, name):
        return self.h[name]


def build_program(debug=None):
    nc = bass.Bass("TRN2", target_bir_lowering=False)

    d_xT = nc.dram_tensor("xT", [D, T], F32, kind="ExternalInput")
    d_xqT = nc.dram_tensor("xqT", [D, TOK], F32, kind="ExternalInput")
    d_xres = nc.dram_tensor("xres", [TOK, D], F32, kind="ExternalInput").ap()
    d_wqT = nc.dram_tensor("wqT", [D, D], F32, kind="ExternalInput")
    d_wkT = nc.dram_tensor("wkT", [D, 512], F32, kind="ExternalInput")
    d_wvT = nc.dram_tensor("wvT", [D, 512], F32, kind="ExternalInput")
    d_woT = nc.dram_tensor("woT", [D, D], F32, kind="ExternalInput").ap()
    d_bqt = nc.dram_tensor("bqt", [128, 16], F32, kind="ExternalInput").ap()
    d_bkt = nc.dram_tensor("bkt", [128, 4], F32, kind="ExternalInput").ap()
    d_bvb = nc.dram_tensor("bvb", [128, 512], F32, kind="ExternalInput").ap()
    d_bob = nc.dram_tensor("bob", [128, D], F32, kind="ExternalInput").ap()
    d_gwT = nc.dram_tensor("gwT", [D, INTER], F32, kind="ExternalInput")
    d_uwT = nc.dram_tensor("uwT", [D, INTER], F32, kind="ExternalInput")
    d_dwT = nc.dram_tensor("dwT", [INTER, D], F32, kind="ExternalInput").ap()
    d_gst = nc.dram_tensor("gst", [128, NIT], F32, kind="ExternalInput").ap()
    d_ust = nc.dram_tensor("ust", [128, NIT], F32, kind="ExternalInput").ap()
    d_dsb = nc.dram_tensor("dsb", [128, D], F32, kind="ExternalInput").ap()
    d_cosK = nc.dram_tensor("cosK", [128, T], F32, kind="ExternalInput").ap()
    d_sinK = nc.dram_tensor("sinK", [128, T], F32, kind="ExternalInput").ap()
    d_cosQ = nc.dram_tensor("cosQ", [128, TOK], F32, kind="ExternalInput").ap()
    d_sinQ = nc.dram_tensor("sinQ", [128, TOK], F32, kind="ExternalInput").ap()
    d_idb = nc.dram_tensor("identb", [128, 128], BF16, kind="ExternalInput").ap()
    d_out = nc.dram_tensor("out", [TOK, D], F32, kind="ExternalOutput").ap()
    d_zb = nc.dram_tensor("zb", [64, 1024], F32)
    d_ydr = nc.dram_tensor("ydr", [TOK, D], F32)

    dbg_outs = {}

    def dbg(name, shape, dt):
        dbg_outs[name] = nc.dram_tensor(name, shape, dt,
                                        kind="ExternalOutput").ap()
        return dbg_outs[name]

    es = contextlib.ExitStack()
    S = Sems(nc, es)
    S.new("x", "xb", "mmKV", "kev", "vev", "sh", "rope",
          "wld", "wcast", "mmQ", "qev", "qsh", "qrope",
          "psS", "psE", "psA", "zr", "zbd", "bc", "aev",
          "wo_x", "wo_mm", "wo_ev", "ln1", "ysp", "f1c", "f1mm", "f1ev",
          "f1w", "f1wc", "gu_mm", "gu_ev", "hdn",
          "f2w", "f2c", "f2mm", "f2ev", "ln2", "outd", "misc",
          "x0", "x1", "x2", "x3", "wq0", "wq1", "wox0", "wox1", "f1w0", "f1w1",
          "f2w0", "f2w1", "f2w2", "zb0", "zb1", "bc0", "bc1", "ysp0", "ysp1",
          "pre", "tabK", "tabQ", "bobd", "dsbd", "wo0", "wo1",
          "ropev", "qropev", "xq0", "xq1", "xqc", "ropea", "tab0", "tab1")

    with es:
        bvb = es.enter_context(nc.sbuf_tensor("s_bvb", [128, 512], F32))
        bkt = es.enter_context(nc.sbuf_tensor("s_bkt", [128, 4], F32))
        bqt = es.enter_context(nc.sbuf_tensor("s_bqt", [128, 16], F32))
        gst = es.enter_context(nc.sbuf_tensor("s_gst", [128, NIT], F32))
        ust = es.enter_context(nc.sbuf_tensor("s_ust", [128, NIT], F32))
        cneg = es.enter_context(nc.sbuf_tensor("cneg", [128, 1], F32))
        epsp = es.enter_context(nc.sbuf_tensor("epsp", [128, 1], F32))
        ident = es.enter_context(nc.sbuf_tensor("ident", [128, 128], BF16))
        lt = es.enter_context(nc.sbuf_tensor("lt", [128, 8], F32))

        for ap_d, t_s in ((d_bvb, bvb), (d_bkt, bkt), (d_bqt, bqt),
                          (d_gst, gst), (d_ust, ust), (d_idb, ident)):
            nc.scalar.dma_start(t_s[:], ap_d[:]).then_inc(*S.inc("pre", 16))
        nc.vector.memset(cneg[:], -C_SHIFT)
        nc.vector.memset(epsp[:], EPS)

        kv_stack = contextlib.ExitStack()
        xq_right = contextlib.ExitStack()
        xq_bf = xq_right.enter_context(
            nc.sbuf_tensor("xq_bf", [128, 16 * TOK], BF16, side="right"))
        KT = kv_stack.enter_context(nc.sbuf_tensor("KT", [128, 4 * T], BF16))
        Vp = kv_stack.enter_context(
            nc.sbuf_tensor("Vp", [128, HKV * NKT * 65], BF16))
        nc.vector.memset(
            bass.AP(Vp, 64, [[HKV * NKT * 65, 128], [65, HKV * NKT]]), 1.0)

        with contextlib.ExitStack() as pa1:
            cosb = [pa1.enter_context(
                nc.sbuf_tensor(f"cosb{i}", [128, 512], F32)) for i in range(2)]
            sinb = [pa1.enter_context(
                nc.sbuf_tensor(f"sinb{i}", [128, 512], F32)) for i in range(2)]
            wk_bf = pa1.enter_context(nc.sbuf_tensor("wk_bf", [128, 8192], BF16))
            wv_bf = pa1.enter_context(nc.sbuf_tensor("wv_bf", [128, 8192], BF16))
            Kstg = pa1.enter_context(nc.sbuf_tensor("Kstg", [128, 2048], F32))
            Ksh = pa1.enter_context(nc.sbuf_tensor("Ksh", [128, 2048], F32))
            xq_st = [pa1.enter_context(
                nc.sbuf_tensor(f"xq_st{i}", [128, TOK], F32)) for i in range(2)]
            xbf = [pa1.enter_context(nc.sbuf_tensor(f"xbf{i}", [128, 512], F32))
                   for i in range(4)]
            xbb = [pa1.enter_context(nc.sbuf_tensor(f"xbb{i}", [128, 512], BF16))
                   for i in range(4)]
            kt1 = [pa1.enter_context(nc.sbuf_tensor(f"kt1{i}", [128, 512], F32))
                   for i in range(2)]
            kt2 = [pa1.enter_context(nc.sbuf_tensor(f"kt2{i}", [128, 512], F32))
                   for i in range(2)]
            psK = [pa1.enter_context(nc.psum_tensor(f"psK{i}", [128, 512], F32))
                   for i in range(4)]
            psV = [pa1.enter_context(nc.psum_tensor(f"psV{i}", [128, 512], F32))
                   for i in range(4)]


            with nc.sbuf_tensor("wst", [128, 4096], F32) as wst:
                for dsrc, bfdst in ((d_wkT, wk_bf), (d_wvT, wv_bf)):
                    for half in range(2):
                        src = bass.AP(dsrc, half * 8 * 128 * 512,
                                      [[512, 128], [128 * 512, 8], [1, 512]])
                        dst = bass.AP(wst, 0, [[4096, 128], [512, 8], [1, 512]])
                        if S.cnt["wcast"] > 0:
                            nc.sync.wait_ge(S["wcast"], S.cnt["wcast"])
                        h_, _ = S.inc("wld", 16); c_ = S.cnt["wld"]
                        nc.sync.dma_start(dst, src).then_inc(h_, 16)
                        nc.vector.wait_ge(S["wld"], c_)
                        nc.vector.tensor_copy(
                            bfdst[:, half * 4096:(half + 1) * 4096], wst[:]
                        ).then_inc(*S.inc("wcast"))
                nc.all_engine_barrier()

            xb_b1 = S.cnt["xb"]

            def kv_load(k):
                bi = k % 4
                tc_, d_ = divmod(k, 16)
                if k >= 4:
                    nc.sync.wait_ge(S["xb"], xb_b1 + k - 3)
                nc.sync.dma_start(
                    xbf[bi][:],
                    d_xT.ap()[d_ * 128:(d_ + 1) * 128,
                              tc_ * 512:(tc_ + 1) * 512],
                ).then_inc(*S.inc(f"x{bi}", 16))

            def kv_cast(k):
                bi = k % 4
                nc.vector.wait_ge(S[f"x{bi}"], S.cnt[f"x{bi}"])
                if k >= 4:
                    nc.vector.wait_ge(S["mmKV"], k - 3)
                nc.vector.tensor_copy(xbb[bi][:], xbf[bi][:]).then_inc(
                    *S.inc("xb"))

            def kv_mm(k):
                bi = k % 4
                tc_, d_ = divmod(k, 16)
                nc.tensor.wait_ge(S["xb"], xb_b1 + k + 1)
                if tc_ > 0 and d_ == 0:
                    nc.tensor.wait_ge(S["kev"], tc_)
                    nc.tensor.wait_ge(S["vev"], tc_)
                for m in range(4):
                    nc.tensor.matmul(
                        psK[m][:],
                        wk_bf[:, d_ * 512 + m * 128:d_ * 512 + (m + 1) * 128],
                        xbb[bi][:], start=(d_ == 0), stop=(d_ == 15))
                mm = None
                for tt in range(4):
                    mm = nc.tensor.matmul(
                        psV[tt][:], xbb[bi][:, tt * 128:(tt + 1) * 128],
                        wv_bf[:, d_ * 512:(d_ + 1) * 512],
                        start=(d_ == 0), stop=(d_ == 15))
                mm.then_inc(*S.inc("mmKV"))

            def kv_evac_rope(tc_):
                nc.scalar.wait_ge(S["mmKV"], (tc_ + 1) * 16)
                if tc_ == 0:
                    nc.scalar.wait_ge(S["pre"], 96)
                else:
                    nc.scalar.wait_ge(S["ropev"], 4 * tc_)
                    nc.scalar.wait_ge(S["sh"], 64 * tc_)
                act = None
                for m in range(4):
                    act = nc.scalar.activation(
                        Kstg[:, m * 512:(m + 1) * 512], psK[m][:], AF.Identity,
                        bias=bkt[:, m:m + 1], scale=1.0)
                act.then_inc(*S.inc("kev"))
                nc.scalar.wait_ge(S["kev"], tc_ + 1)
                for gg in range(2):
                    for hswap in range(2):
                        so, do_ = (32, 0) if hswap == 0 else (0, 32)
                        nc.scalar.dma_start(
                            Ksh[gg * 64 + do_:gg * 64 + do_ + 32, :],
                            Kstg[gg * 64 + so:gg * 64 + so + 32, :],
                        ).then_inc(*S.inc("sh", 16))
                nc.vector.wait_ge(S["mmKV"], (tc_ + 1) * 16)
                if tc_ == 0:
                    nc.vector.wait_ge(S["pre"], 96)
                tt_last = None
                for tt in range(4):
                    kt_g = tc_ * 4 + tt
                    dst = bass.AP(Vp, kt_g * 65,
                                  [[HKV * NKT * 65, 128], [2080, HKV], [1, 64]])
                    src = bass.AP(psV[tt], 0, [[512, 128], [64, HKV], [1, 64]])
                    tt_last = nc.vector.tensor_tensor(dst, src, bvb[:], ALU.add)
                tt_last.then_inc(*S.inc("vev"))
                nc.vector.wait_ge(S["sh"], 64 * (tc_ + 1))
                nc.vector.wait_ge(S[f"tab{tc_ % 2}"], S.cnt[f"tab{tc_ % 2}"])
                for m in range(4):
                    gm = 4 * tc_ + m
                    if gm >= 2:
                        nc.vector.wait_ge(S["ropea"], gm - 1)
                    nc.vector.tensor_tensor(
                        kt1[m % 2][:], Ksh[:, m * 512:(m + 1) * 512],
                        sinb[tc_ % 2][:], ALU.mult)
                    nc.vector.tensor_tensor(
                        kt2[m % 2][:], Kstg[:, m * 512:(m + 1) * 512],
                        cosb[tc_ % 2][:], ALU.mult
                    ).then_inc(*S.inc("ropev"))
                last = None
                for m in range(4):
                    gm = 4 * tc_ + m
                    nc.gpsimd.wait_ge(S["ropev"], gm + 1)
                    last = nc.gpsimd.tensor_tensor(
                        KT[:, m * T + tc_ * 512:m * T + (tc_ + 1) * 512],
                        kt1[m % 2][:], kt2[m % 2][:], ALU.add
                    ).then_inc(*S.inc("ropea"))

            def tab_load(tc_):
                bi = tc_ % 2
                if tc_ >= 2:
                    nc.gpsimd.wait_ge(S["ropev"], 4 * (tc_ - 1))
                nc.gpsimd.dma_start(cosb[bi][:],
                                    d_cosK[:, tc_ * 512:(tc_ + 1) * 512]
                                    ).then_inc(*S.inc(f"tab{bi}", 16))
                nc.gpsimd.dma_start(sinb[bi][:],
                                    d_sinK[:, tc_ * 512:(tc_ + 1) * 512]
                                    ).then_inc(*S.inc(f"tab{bi}", 16))

            def xq_load(dq):
                bi = dq % 2
                if dq >= 2:
                    nc.gpsimd.wait_ge(S["xqc"], dq - 1)
                nc.gpsimd.dma_start(xq_st[bi][:],
                                    d_xqT.ap()[dq * 128:(dq + 1) * 128, :]
                                    ).then_inc(*S.inc(f"xq{bi}", 16))

            def xq_cast(dq):
                bi = dq % 2
                nc.vector.wait_ge(S[f"xq{bi}"], S.cnt[f"xq{bi}"])
                nc.vector.tensor_copy(xq_bf[:, dq * TOK:(dq + 1) * TOK],
                                      xq_st[bi][:]).then_inc(*S.inc("xqc"))

            tab_load(0)
            tab_load(1)
            for k in range(4):
                kv_load(k)
                kv_cast(k)
            xq_load(0)
            xq_load(1)
            for g in range(NTC * 16):
                tc, d = divmod(g, 16)
                if g + 4 < NTC * 16:
                    kv_load(g + 4)
                    kv_cast(g + 4)
                kv_mm(g)
                if d == 7 and tc < 8:
                    xq_cast(2 * tc)
                    if 2 * tc + 2 < 16:
                        xq_load(2 * tc + 2)
                if d == 15:
                    kv_evac_rope(tc)
                    if tc + 2 < NTC:
                        tab_load(tc + 2)
                    xq_cast(2 * tc + 1)
                    if 2 * tc + 3 < 16:
                        xq_load(2 * tc + 3)
            nc.all_engine_barrier()
            if debug == "pa1":
                o1 = dbg("dbg_KT", [128, 4 * T], BF16)
                o2 = dbg("dbg_Vp", [128, HKV * NKT * 65], BF16)
                nc.sync.dma_start(o1[:], KT[:]).then_inc(*S.inc("outd", 16))
                nc.sync.dma_start(o2[:], Vp[:]).then_inc(*S.inc("outd", 16))
                nc.sync.wait_ge(S["outd"], S.cnt["outd"])
                nc.all_engine_barrier()

        q_stack = contextlib.ExitStack()
        QT2 = q_stack.enter_context(nc.sbuf_tensor("QT2", [128, 16 * TOK], BF16))
        with contextlib.ExitStack() as pa2:
            cosQ = pa2.enter_context(nc.sbuf_tensor("s_cosQ", [128, TOK], F32))
            sinQ = pa2.enter_context(nc.sbuf_tensor("s_sinQ", [128, TOK], F32))
            wqs_st = [pa2.enter_context(
                nc.sbuf_tensor(f"wqs_st{i}", [128, 2048], F32)) for i in range(2)]
            wqs_bf = [pa2.enter_context(
                nc.sbuf_tensor(f"wqs_bf{i}", [128, 2048], BF16)) for i in range(2)]
            Qstg = [pa2.enter_context(
                nc.sbuf_tensor(f"Qstg{i}", [128, TOK], F32)) for i in range(2)]
            Qsh = [pa2.enter_context(
                nc.sbuf_tensor(f"Qsh{i}", [128, TOK], F32)) for i in range(2)]
            qt1 = [pa2.enter_context(nc.sbuf_tensor(f"qt1{i}", [128, TOK], F32))
                   for i in range(2)]
            qt2 = [pa2.enter_context(nc.sbuf_tensor(f"qt2{i}", [128, TOK], F32))
                   for i in range(2)]
            psQ = [pa2.enter_context(nc.psum_tensor(f"psQ{i}", [128, 512], F32))
                   for i in range(8)]

            nc.scalar.dma_start(cosQ[:], d_cosQ[:]).then_inc(*S.inc("tabQ", 16))
            nc.scalar.dma_start(sinQ[:], d_sinQ[:]).then_inc(*S.inc("tabQ", 16))
            wc_base = S.cnt["wcast"]
            mq_base = S.cnt["mmQ"]

            def q_load(j):
                bi = j % 2
                if j >= 2:
                    nc.sync.wait_ge(S["wcast"], wc_base + j - 1)
                for half in range(2):
                    src = bass.AP(d_wqT, half * 8 * 128 * D + j * 128,
                                  [[D, 128], [128 * D, 8], [1, 128]])
                    dst = bass.AP(wqs_st[bi], half * 1024,
                                  [[2048, 128], [128, 8], [1, 128]])
                    nc.sync.dma_start(dst, src).then_inc(*S.inc(f"wq{bi}", 16))

            def q_cast(j):
                bi = j % 2
                nc.vector.wait_ge(S[f"wq{bi}"], S.cnt[f"wq{bi}"])
                if j >= 2:
                    nc.vector.wait_ge(S["mmQ"], mq_base + j - 1)
                nc.vector.tensor_copy(wqs_bf[bi][:], wqs_st[bi][:]
                                      ).then_inc(*S.inc("wcast"))

            def q_comp(j):
                bi = j % 2
                sbi = j % 2
                nc.tensor.wait_ge(S["wcast"], wc_base + j + 1)
                if j == 0:
                    nc.tensor.wait_ge(S["xqc"], 16)
                pA, pB = psQ[(j % 4) * 2], psQ[(j % 4) * 2 + 1]
                if j >= 4:
                    nc.tensor.wait_ge(S["qev"], j - 3)
                mmq = None
                for d in range(16):
                    nc.tensor.matmul(pA[:],
                                     wqs_bf[bi][:, d * 128:(d + 1) * 128],
                                     xq_bf[:, d * TOK:d * TOK + 512],
                                     start=(d == 0), stop=(d == 15))
                    mmq = nc.tensor.matmul(
                        pB[:], wqs_bf[bi][:, d * 128:(d + 1) * 128],
                        xq_bf[:, d * TOK + 512:(d + 1) * TOK],
                        start=(d == 0), stop=(d == 15))
                mmq.then_inc(*S.inc("mmQ"))
                nc.scalar.wait_ge(S["mmQ"], mq_base + j + 1)
                if j == 0:
                    nc.scalar.wait_ge(S["pre"], 96)
                if j >= 2:
                    nc.scalar.wait_ge(S["qrope"], j - 1)
                nc.scalar.activation(Qstg[sbi][:, 0:512], pA[:], AF.Identity,
                                     bias=bqt[:, j:j + 1], scale=1.0)
                nc.scalar.activation(Qstg[sbi][:, 512:1024], pB[:],
                                     AF.Identity, bias=bqt[:, j:j + 1],
                                     scale=1.0).then_inc(*S.inc("qev"))
                nc.scalar.wait_ge(S["qev"], j + 1)
                if j >= 1:
                    nc.scalar.wait_ge(S["qrope"], j)
                for gg in range(2):
                    for hswap in range(2):
                        so, do_ = (32, 0) if hswap == 0 else (0, 32)
                        nc.scalar.dma_start(
                            Qsh[sbi][gg * 64 + do_:gg * 64 + do_ + 32, :],
                            Qstg[sbi][gg * 64 + so:gg * 64 + so + 32, :],
                        ).then_inc(*S.inc("qsh", 16))
                nc.vector.wait_ge(S["qsh"], S.cnt["qsh"])
                if j == 0:
                    nc.vector.wait_ge(S["tabQ"], 32)
                if j >= 2:
                    nc.vector.wait_ge(S["qrope"], j - 1)
                nc.vector.tensor_tensor(qt1[sbi][:], Qsh[sbi][:], sinQ[:],
                                        ALU.mult)
                nc.vector.tensor_tensor(qt2[sbi][:], Qstg[sbi][:], cosQ[:],
                                        ALU.mult).then_inc(*S.inc("qropev"))
                nc.gpsimd.wait_ge(S["qropev"], j + 1)
                last = None
                for H, rsrc in ((2 * j, 0), (2 * j + 1, 64)):
                    blk = 4 * (H // 8) + (H % 4)
                    r0 = 0 if (H % 8) < 4 else 64
                    last = nc.gpsimd.tensor_tensor(
                        QT2[r0:r0 + 64, blk * TOK:(blk + 1) * TOK],
                        qt1[sbi][rsrc:rsrc + 64, :], qt2[sbi][rsrc:rsrc + 64, :],
                        ALU.add)
                last.then_inc(*S.inc("qrope"))

            q_load(0)
            q_cast(0)
            q_load(1)
            q_cast(1)
            for j in range(16):
                if j + 2 < 16:
                    q_load(j + 2)
                    q_cast(j + 2)
                q_comp(j)
            nc.all_engine_barrier()
            if debug == "pa2":
                o1 = dbg("dbg_QT2", [128, 16 * TOK], BF16)
                nc.sync.dma_start(o1[:], QT2[:]).then_inc(*S.inc("outd", 16))
                nc.sync.wait_ge(S["outd"], S.cnt["outd"])
                nc.all_engine_barrier()

        xq_right.close()
        attn_right = contextlib.ExitStack()
        attnT = attn_right.enter_context(
            nc.sbuf_tensor("attnT", [128, 16 * TOK], BF16, side="right"))
        woA = attn_right.enter_context(
            nc.sbuf_tensor("woA", [128, 10 * D], BF16, side="right"))
        with contextlib.ExitStack() as psx:
            wo_st = [psx.enter_context(
                nc.sbuf_tensor(f"wo_st{i}", [128, 1024], F32)) for i in range(2)]
            Sps = [psx.enter_context(nc.psum_tensor(f"Sps{i}", [128, 1024], F32))
                   for i in range(2)]
            Aps = [psx.enter_context(nc.psum_tensor(f"Aps{i}", [65, 512], F32))
                   for i in range(4)]
            wT = [psx.enter_context(nc.sbuf_tensor(f"wT{i}", [128, 1024], BF16))
                  for i in range(4)]
            zrec = [psx.enter_context(nc.sbuf_tensor(f"zrec{i}", [1, 1024], F32))
                    for i in range(2)]
            bcs = [psx.enter_context(nc.sbuf_tensor(f"bcs{i}", [128, 1024], F32))
                   for i in range(2)]

            seq = [(m, i, qb) for m in range(4) for i in range(4)
                   for qb in range(2)]

            def ps_S(itx, kt):
                m, i, qb = seq[itx]
                blk = 4 * m + i
                sbi = kt % 2
                wbi = kt % 4
                g = itx * NKT + kt
                if g >= 2:
                    nc.tensor.wait_ge(S["psE"], g - 1)
                nc.tensor.matmul(
                    Sps[sbi][:, 0:512],
                    KT[0:64, m * T + kt * 128:m * T + (kt + 1) * 128],
                    QT2[0:64, blk * TOK + qb * 512:blk * TOK + (qb + 1) * 512],
                    start=True, stop=True, tile_position=(0, 0))
                nc.tensor.matmul(
                    Sps[sbi][:, 512:1024],
                    KT[64:128, m * T + kt * 128:m * T + (kt + 1) * 128],
                    QT2[64:128, blk * TOK + qb * 512:blk * TOK + (qb + 1) * 512],
                    start=True, stop=True, tile_position=(64, 0)
                ).then_inc(*S.inc("psS"))
                nc.scalar.wait_ge(S["psS"], g + 1)
                if g >= 4:
                    nc.scalar.wait_ge(S["psA"], g - 3)
                nc.scalar.activation(
                    wT[wbi][:], Sps[sbi][:], AF.Exp,
                    bias=cneg[:], scale=0.125).then_inc(*S.inc("psE"))

            def ps_AV(itx, kt):
                m, i, qb = seq[itx]
                kve, kvo = 2 * m, 2 * m + 1
                aset = itx % 2
                A0, A1 = Aps[aset * 2], Aps[aset * 2 + 1]
                wbi = kt % 4
                g = itx * NKT + kt
                if kt == 0 and itx >= 2:
                    nc.tensor.wait_ge(S["aev"], itx - 1)
                nc.tensor.wait_ge(S["psE"], g + 1)
                nc.tensor.matmul(
                    A0[:],
                    Vp[:, kve * 2080 + kt * 65:kve * 2080 + kt * 65 + 65],
                    wT[wbi][:, 0:512],
                    start=(kt == 0), stop=(kt == NKT - 1))
                nc.tensor.matmul(
                    A1[:],
                    Vp[:, kvo * 2080 + kt * 65:kvo * 2080 + kt * 65 + 65],
                    wT[wbi][:, 512:1024],
                    start=(kt == 0), stop=(kt == NKT - 1)
                ).then_inc(*S.inc("psA"))

            def ps_epilogue(itx):
                m, i, qb = seq[itx]
                He, Ho = 8 * m + i, 8 * m + 4 + i
                aset = itx % 2
                A0, A1 = Aps[aset * 2], Aps[aset * 2 + 1]
                bset = itx % 2
                nc.vector.wait_ge(S["psA"], (itx + 1) * NKT)
                if itx >= 2:
                    nc.vector.wait_ge(S[f"zb{bset}"], S.cnt[f"zb{bset}"])
                nc.vector.reciprocal(zrec[bset][0:1, 0:512], A0[64:65, :])
                nc.vector.reciprocal(zrec[bset][0:1, 512:1024],
                                     A1[64:65, :]).then_inc(*S.inc("zr"))
                nc.gpsimd.wait_ge(S["zr"], S.cnt["zr"])
                h_, _ = S.inc(f"zb{bset}", 16); c_ = S.cnt[f"zb{bset}"]
                nc.gpsimd.dma_start(
                    bass.AP(d_zb, itx * 1024, [[1024, 1], [1, 1024]]),
                    zrec[bset][0:1, :]).then_inc(h_, 16)
                nc.gpsimd.wait_ge(S[f"zb{bset}"], c_)
                if itx >= 2:
                    nc.gpsimd.wait_ge(S["aev"], itx - 1)
                h_, _ = S.inc(f"bc{bset}", 16); c_ = S.cnt[f"bc{bset}"]
                nc.gpsimd.dma_start(
                    bcs[bset][:],
                    bass.AP(d_zb, itx * 1024, [[0, 128], [1, 1024]]),
                ).then_inc(h_, 16)
                nc.vector.wait_ge(S[f"bc{bset}"], c_)
                lastev = None
                for H, A_, co in ((He, A0, 0), (Ho, A1, 512)):
                    bj = H // 2
                    r0 = (H % 2) * 64
                    lastev = nc.vector.tensor_tensor(
                        attnT[r0:r0 + 64,
                              bj * TOK + qb * 512:bj * TOK + (qb + 1) * 512],
                        A_[0:64, :], bcs[bset][0:64, co:co + 512], ALU.mult)
                lastev.then_inc(*S.inc("aev"))

            woc_base = S.cnt["wcast"]

            def wo_preload(k):
                bi = k % 2
                j, half = divmod(k, 2)
                if k >= 2:
                    nc.sync.wait_ge(S["wcast"], woc_base + k - 1)
                nc.sync.dma_start(
                    wo_st[bi][:],
                    d_woT[j * 128:(j + 1) * 128,
                          half * 1024:(half + 1) * 1024]
                ).then_inc(*S.inc(f"wo{bi}", 16))
                nc.vector.wait_ge(S[f"wo{bi}"], S.cnt[f"wo{bi}"])
                nc.vector.tensor_copy(
                    woA[:, j * D + half * 1024:j * D + (half + 1) * 1024],
                    wo_st[bi][:]).then_inc(*S.inc("wcast"))

            GPS = len(seq) * NKT
            ps_S(0, 0)
            ps_S(0, 1)
            for g in range(GPS):
                itx, kt = divmod(g, NKT)
                ps_AV(itx, kt)
                if g + 2 < GPS:
                    ps_S((g + 2) // NKT, (g + 2) % NKT)
                if kt == NKT - 1:
                    ps_epilogue(itx)
                    if itx < 10:
                        wo_preload(2 * itx)
                        wo_preload(2 * itx + 1)
            nc.all_engine_barrier()
            if debug == "ps":
                o1 = dbg("dbg_attnT", [128, 16 * TOK], BF16)
                nc.sync.dma_start(o1[:], attnT[:]).then_inc(*S.inc("outd", 16))
                nc.sync.wait_ge(S["outd"], S.cnt["outd"])
                nc.all_engine_barrier()

        q_stack.close()
        kv_stack.close()

        yt_stack = contextlib.ExitStack()
        yT = yt_stack.enter_context(nc.sbuf_tensor("yT", [128, 16 * TOK], BF16))
        with contextlib.ExitStack() as wo:
            woB = wo.enter_context(nc.sbuf_tensor("woB", [128, 6 * D], BF16))
            with contextlib.ExitStack() as pre:
                wo_st2 = [pre.enter_context(
                    nc.sbuf_tensor(f"wo_st2{i}", [128, 1024], F32))
                    for i in range(2)]
                wc_base = S.cnt["wcast"]
                for k in range(12):
                    bi = k % 2
                    j, half = divmod(k, 2)
                    if k >= 2:
                        nc.sync.wait_ge(S["wcast"], wc_base + k - 1)
                    nc.sync.dma_start(
                        wo_st2[bi][:],
                        d_woT[(10 + j) * 128:(11 + j) * 128,
                              half * 1024:(half + 1) * 1024]
                    ).then_inc(*S.inc(f"wo{bi}", 16))
                    nc.vector.wait_ge(S[f"wo{bi}"], S.cnt[f"wo{bi}"])
                    nc.vector.tensor_copy(
                        woB[:, j * D + half * 1024:j * D + (half + 1) * 1024],
                        wo_st2[bi][:]).then_inc(*S.inc("wcast"))
                nc.vector.drain().then_inc(*S.inc("wcast"))
                woB_target = S.cnt["wcast"]
            bob = wo.enter_context(nc.sbuf_tensor("s_bob", [128, D], F32))
            xr = [wo.enter_context(nc.sbuf_tensor(f"xr{i}", [128, D], F32))
                  for i in range(2)]
            wtmp = [wo.enter_context(nc.sbuf_tensor(f"wtmp{i}", [128, 512], F32))
                    for i in range(4)]
            ytile = [wo.enter_context(nc.sbuf_tensor(f"ytile{i}", [128, D], F32))
                     for i in range(2)]
            ybf = [wo.enter_context(nc.sbuf_tensor(f"ybf{i}", [128, D], BF16))
                   for i in range(2)]
            junk = wo.enter_context(nc.sbuf_tensor("junk", [128, D], F32))
            psW = [wo.enter_context(nc.psum_tensor(f"psW{i}", [128, 512], F32))
                   for i in range(8)]

            nc.scalar.wait_ge(S["wcast"], woB_target)
            nc.scalar.dma_start(bob[:], d_bob[:]).then_inc(*S.inc("bobd", 16))
            nc.scalar.wait_ge(S["bobd"], 16)
            nc.all_engine_barrier()

            def emit_transposes(qt):
                par = qt % 2
                ws = par * 4
                if qt == 0:
                    nc.tensor.wait_ge(S["pre"], 96)
                nc.tensor.wait_ge(S["f1c"], qt + 1)
                lastt = None
                for k in range(16):
                    kk = k % 8
                    pv = bass.AP(psW[ws + k // 8], kk * 64,
                                 [[512, 128], [1, 64]]).bitcast(BF16)
                    lastt = nc.tensor.transpose(
                        pv, ybf[par][:, k * 128:(k + 1) * 128], ident[:])
                lastt.then_inc(*S.inc("f1mm"))
                nc.vector.wait_ge(S["f1mm"], S.cnt["f1mm"])
                for half in range(2):
                    pv = bass.AP(psW[ws + half], 0,
                                 [[512, 128], [1, 512]]).bitcast(BF16)
                    dst = bass.AP(yT, (half * 8) * TOK + qt * 128,
                                  [[16 * TOK, 128], [TOK, 8], [1, 128]])
                    ev = nc.vector.tensor_copy(dst, pv)
                ev.then_inc(*S.inc("f1ev"))

            for qt in range(8):
                par = qt % 2
                ws = par * 4
                if qt >= 2:
                    nc.sync.wait_ge(S["wo_ev"], qt - 1)
                h_, _ = S.inc(f"wox{par}", 16); c_ = S.cnt[f"wox{par}"]
                nc.sync.dma_start(xr[par][:],
                                  d_xres[qt * 128:(qt + 1) * 128, :]
                                  ).then_inc(h_, 16)
                if qt >= 2:
                    nc.tensor.wait_ge(S["wo_ev"], qt - 1)
                    nc.tensor.wait_ge(S["f1ev"], qt - 1)
                mmwo = None
                for j in range(16):
                    if qt == 0 and j >= 10:
                        nc.tensor.wait_ge(S["wcast"],
                                          woB_target - (15 - j) * 2)
                    wsrc = (woA[:, j * D:(j + 1) * D] if j < 10
                            else woB[:, (j - 10) * D:(j - 9) * D])
                    for ds in range(4):
                        mmwo = nc.tensor.matmul(
                            psW[ws + ds][:],
                            attnT[:, j * TOK + qt * 128:j * TOK + (qt + 1) * 128],
                            wsrc[:, ds * 512:(ds + 1) * 512],
                            start=(j == 0), stop=(j == 15))
                mmwo.then_inc(*S.inc("wo_mm"))
                nc.vector.wait_ge(S["wo_mm"], qt + 1)
                nc.vector.wait_ge(S[f"wox{par}"], c_)
                if qt >= 2:
                    nc.vector.wait_ge(S[f"ysp{par}"],
                                      S.cnt[f"ysp{par}"])
                for ds in range(4):
                    nc.vector.tensor_tensor(
                        wtmp[ds][:], psW[ws + ds][:],
                        xr[par][:, ds * 512:(ds + 1) * 512], ALU.add)
                nc.vector.drain()
                lastv = None
                for ds in range(4):
                    lastv = nc.vector.tensor_tensor(
                        ytile[par][:, ds * 512:(ds + 1) * 512],
                        wtmp[ds][:], bob[:, ds * 512:(ds + 1) * 512], ALU.add)
                lastv.then_inc(*S.inc("wo_ev"))
                rseg = ytile[par][:]
                nc.scalar.wait_ge(S["wo_ev"], qt + 1)
                nc.scalar.activation(junk[:], rseg, AF.Square,
                                     accum_out=lt[:, 1:2])
                nc.scalar.drain()
                nc.scalar.activation(junk[:], rseg, AF.Copy,
                                     accum_out=lt[:, 0:1])
                nc.scalar.drain()
                nc.scalar.activation(lt[:, 2:3], lt[:, 0:1], AF.Copy,
                                     scale=-1.0 / D)
                nc.scalar.drain()
                nc.scalar.activation(lt[:, 3:4], lt[:, 2:3], AF.Square)
                nc.scalar.drain()
                nc.scalar.activation(lt[:, 4:5], lt[:, 1:2], AF.Copy,
                                     scale=1.0 / D).then_inc(*S.inc("ln1"))
                nc.vector.wait_ge(S["ln1"], S.cnt["ln1"])
                nc.vector.tensor_tensor(lt[:, 5:6], lt[:, 4:5], lt[:, 3:4],
                                        ALU.subtract)
                nc.vector.drain()
                nc.vector.tensor_tensor(lt[:, 6:7], lt[:, 5:6], epsp[:],
                                        ALU.add).then_inc(*S.inc("ln1"))
                nc.scalar.wait_ge(S["ln1"], S.cnt["ln1"])
                nc.scalar.activation(lt[:, 7:8], lt[:, 6:7], AF.Ln)
                nc.scalar.drain()
                nc.scalar.activation(lt[:, 6:7], lt[:, 7:8], AF.Exp,
                                     scale=-0.5).then_inc(*S.inc("ln1"))
                nc.vector.wait_ge(S["ln1"], S.cnt["ln1"])
                nc.vector.tensor_scalar_mul(lt[:, 7:8], lt[:, 2:3],
                                            lt[:, 6:7]).then_inc(*S.inc("ln1"))
                nc.scalar.wait_ge(S["ln1"], S.cnt["ln1"])
                nc.scalar.activation(rseg, rseg, AF.Identity, bias=lt[:, 7:8],
                                     scale=lt[:, 6:7]).then_inc(*S.inc("ln1"))
                nc.sync.wait_ge(S["ln1"], S.cnt["ln1"])
                nc.sync.dma_start(d_ydr.ap()[qt * 128:(qt + 1) * 128, :], rseg
                                  ).then_inc(*S.inc(f"ysp{par}", 16))
                nc.vector.wait_ge(S["ln1"], S.cnt["ln1"])
                if qt >= 2:
                    nc.vector.wait_ge(S["f1mm"], qt - 1)
                nc.vector.tensor_copy(ybf[par][:], rseg).then_inc(*S.inc("f1c"))
                emit_transposes(qt)
            nc.sync.wait_ge(S["ysp0"], S.cnt["ysp0"])
            nc.sync.wait_ge(S["ysp1"], S.cnt["ysp1"])
            nc.all_engine_barrier()
            if debug == "wo":
                o1 = dbg("dbg_yT", [128, 16 * TOK], BF16)
                nc.sync.dma_start(o1[:], yT[:]).then_inc(*S.inc("outd", 16))
                nc.sync.wait_ge(S["outd"], S.cnt["outd"])
                nc.all_engine_barrier()

        attn_right.close()
        f1_right = contextlib.ExitStack()
        hdnT = f1_right.enter_context(
            nc.sbuf_tensor("hdnT", [128, NIT * TOK], BF16, side="right"))
        with contextlib.ExitStack() as f1:
            g_st = [f1.enter_context(nc.sbuf_tensor(f"g_st{i}", [128, D], F32))
                    for i in range(2)]
            g_bf = [f1.enter_context(nc.sbuf_tensor(f"g_bf{i}", [128, D], BF16))
                    for i in range(2)]
            u_st = [f1.enter_context(nc.sbuf_tensor(f"u_st{i}", [128, D], F32))
                    for i in range(2)]
            u_bf = [f1.enter_context(nc.sbuf_tensor(f"u_bf{i}", [128, D], BF16))
                    for i in range(2)]
            sg = [f1.enter_context(nc.sbuf_tensor(f"sg{i}", [128, 1024], BF16))
                  for i in range(2)]
            su = [f1.enter_context(nc.sbuf_tensor(f"su{i}", [128, 1024], BF16))
                  for i in range(2)]
            psG = [f1.enter_context(nc.psum_tensor(f"psG{i}", [128, 1024], F32))
                   for i in range(4)]

            f1w_base = S.cnt["f1wc"]
            gu_base = S.cnt["gu_mm"]
            hdn_base = S.cnt["hdn"]
            ev_base = S.cnt["gu_ev"]

            def f1_load(k):
                bi = k % 2
                if k >= 2:
                    nc.sync.wait_ge(S["f1wc"], f1w_base + k - 1)
                for dsrc, st_ in ((d_gwT, g_st[bi]), (d_uwT, u_st[bi])):
                    for half in range(2):
                        src = bass.AP(dsrc, half * 8 * 128 * INTER + k * 128,
                                      [[INTER, 128], [128 * INTER, 8], [1, 128]])
                        dst = bass.AP(st_, half * 1024,
                                      [[D, 128], [128, 8], [1, 128]])
                        nc.sync.dma_start(dst, src).then_inc(
                            *S.inc(f"f1w{bi}", 16))

            def f1_cast(k):
                bi = k % 2
                nc.vector.wait_ge(S[f"f1w{bi}"], S.cnt[f"f1w{bi}"])
                if k >= 2:
                    nc.vector.wait_ge(S["gu_mm"], gu_base + k - 1)
                nc.vector.tensor_copy(g_bf[bi][:], g_st[bi][:])
                nc.vector.tensor_copy(u_bf[bi][:], u_st[bi][:]
                                      ).then_inc(*S.inc("f1wc"))

            def f1_comp(k):
                bi = k % 2
                pg, pu = psG[bi * 2], psG[bi * 2 + 1]
                nc.tensor.wait_ge(S["f1wc"], f1w_base + k + 1)
                if k >= 2:
                    nc.tensor.wait_ge(S["hdn"], hdn_base + k - 1)
                lastm = None
                for hc in range(16):
                    for pp, wbf in ((pg, g_bf[bi]), (pu, u_bf[bi])):
                        for th in range(2):
                            lastm = nc.tensor.matmul(
                                pp[:, th * 512:(th + 1) * 512],
                                wbf[:, hc * 128:(hc + 1) * 128],
                                yT[:, hc * TOK + th * 512:
                                   hc * TOK + (th + 1) * 512],
                                start=(hc == 0), stop=(hc == 15))
                lastm.then_inc(*S.inc("gu_mm"))
                nc.scalar.wait_ge(S["gu_mm"], gu_base + k + 1)
                if k == 0:
                    nc.scalar.wait_ge(S["pre"], 96)
                if k >= 2:
                    nc.scalar.wait_ge(S["hdn"], hdn_base + k - 1)
                nc.scalar.activation(sg[bi][:], pg[:], AF.Silu,
                                     scale=gst[:, k:k + 1])
                nc.scalar.activation(su[bi][:], pu[:], AF.Copy,
                                     scale=ust[:, k:k + 1]
                                     ).then_inc(*S.inc("gu_ev"))
                nc.gpsimd.wait_ge(S["gu_ev"], ev_base + k + 1)
                nc.gpsimd.tensor_tensor(hdnT[:, k * TOK:(k + 1) * TOK],
                                        sg[bi][:], su[bi][:], ALU.mult
                                        ).then_inc(*S.inc("hdn"))

            f1_load(0)
            f1_cast(0)
            f1_load(1)
            f1_cast(1)
            for k in range(NIT):
                if k + 2 < NIT:
                    f1_load(k + 2)
                    f1_cast(k + 2)
                f1_comp(k)
            nc.all_engine_barrier()
            if debug == "ffn1":
                o1 = dbg("dbg_hdnT", [128, NIT * TOK], BF16)
                nc.sync.dma_start(o1[:], hdnT[:]).then_inc(*S.inc("outd", 16))
                nc.sync.wait_ge(S["outd"], S.cnt["outd"])
                nc.all_engine_barrier()

        yt_stack.close()
        with contextlib.ExitStack() as f2:
            res2 = f2.enter_context(nc.sbuf_tensor("res2", [128, 8 * D], F32))
            dsb = f2.enter_context(nc.sbuf_tensor("s_dsb", [128, D], F32))
            lt2 = f2.enter_context(nc.sbuf_tensor("lt2", [128, 64], F32))
            dw_st = [f2.enter_context(
                nc.sbuf_tensor(f"dw_st{i}", [128, 512], F32)) for i in range(3)]
            dw_bf = [f2.enter_context(
                nc.sbuf_tensor(f"dw_bf{i}", [128, 512], BF16)) for i in range(3)]
            dtmp = [f2.enter_context(
                nc.sbuf_tensor(f"dtmp{i}", [128, 512], F32)) for i in range(8)]
            psD = [f2.enter_context(nc.psum_tensor(f"psD{i}", [128, 512], F32))
                   for i in range(8)]

            nc.scalar.wait_ge(S["ysp0"], S.cnt["ysp0"])
            nc.scalar.wait_ge(S["ysp1"], S.cnt["ysp1"])
            nc.scalar.dma_start(dsb[:], d_dsb[:]).then_inc(*S.inc("dsbd", 16))
            for qt in range(8):
                nc.scalar.dma_start(res2[:, qt * D:(qt + 1) * D],
                                    d_ydr.ap()[qt * 128:(qt + 1) * 128, :]
                                    ).then_inc(*S.inc("dsbd", 16))
            nc.scalar.wait_ge(S["dsbd"], 144)
            nc.all_engine_barrier()

            f2c_base = S.cnt["f2c"]
            f2m_base = S.cnt["f2mm"]

            def f2_load(g):
                bi = g % 3
                hs_, k_ = divmod(g, NIT)
                if g >= 3:
                    nc.sync.wait_ge(S["f2c"], f2c_base + g - 2)
                nc.sync.dma_start(
                    dw_st[bi][:],
                    d_dwT[k_ * 128:(k_ + 1) * 128,
                          hs_ * 512:(hs_ + 1) * 512]
                ).then_inc(*S.inc(f"f2w{bi}", 16))

            def f2_cast(g):
                bi = g % 3
                nc.vector.wait_ge(S[f"f2w{bi}"], S.cnt[f"f2w{bi}"])
                if g >= 3:
                    nc.vector.wait_ge(S["f2mm"], f2m_base + g - 2)
                nc.vector.tensor_copy(dw_bf[bi][:], dw_st[bi][:]
                                      ).then_inc(*S.inc("f2c"))

            def f2_comp(g):
                bi = g % 3
                hs_, k_ = divmod(g, NIT)
                nc.tensor.wait_ge(S["f2c"], f2c_base + g + 1)
                if k_ == 0 and hs_ > 0:
                    nc.tensor.wait_ge(S["f2ev"], hs_)
                lastm = None
                for tt in range(8):
                    lastm = nc.tensor.matmul(
                        psD[tt][:],
                        hdnT[:, k_ * TOK + tt * 128:k_ * TOK + (tt + 1) * 128],
                        dw_bf[bi][:],
                        start=(k_ == 0), stop=(k_ == NIT - 1))
                lastm.then_inc(*S.inc("f2mm"))

            def f2_evac(hs_):
                nc.vector.wait_ge(S["f2mm"], f2m_base + (hs_ + 1) * NIT)
                for tt in range(8):
                    nc.vector.tensor_tensor(dtmp[tt][:], psD[tt][:],
                                            dsb[:, hs_ * 512:(hs_ + 1) * 512],
                                            ALU.mult)
                nc.vector.drain()
                laste = None
                for tt in range(8):
                    seg = res2[:, tt * D + hs_ * 512:tt * D + (hs_ + 1) * 512]
                    laste = nc.vector.tensor_tensor(seg, seg, dtmp[tt][:],
                                                    ALU.add)
                laste.then_inc(*S.inc("f2ev"))

            NG = 4 * NIT
            for g0 in range(3):
                f2_load(g0)
                f2_cast(g0)
            for g in range(NG):
                if g + 3 < NG:
                    f2_load(g + 3)
                    f2_cast(g + 3)
                f2_comp(g)
                if g % NIT == NIT - 1:
                    f2_evac(g // NIT)
            nc.all_engine_barrier()

            qstat = None
            for qt in range(8):
                rseg = res2[:, qt * D:(qt + 1) * D]
                qstat = nc.scalar.activation(dsb[:], rseg, AF.Square,
                                             accum_out=lt2[:, 8 + qt:9 + qt])
                nc.vector.tensor_reduce(lt2[:, qt:qt + 1], rseg,
                                        mybir.AxisListType.X, ALU.add)
            qstat.then_inc(*S.inc("ln2"))
            nc.vector.drain()
            nc.vector.wait_ge(S["ln2"], S.cnt["ln2"])
            nc.vector.tensor_scalar_mul(lt2[:, 16:24], lt2[:, 0:8], -1.0 / D)
            nc.vector.tensor_scalar_mul(lt2[:, 24:32], lt2[:, 8:16], 1.0 / D)
            nc.vector.drain()
            nc.vector.tensor_tensor(lt2[:, 32:40], lt2[:, 16:24],
                                    lt2[:, 16:24], ALU.mult)
            nc.vector.drain()
            nc.vector.tensor_tensor(lt2[:, 40:48], lt2[:, 24:32],
                                    lt2[:, 32:40], ALU.subtract)
            nc.vector.drain()
            nc.vector.tensor_scalar_add(lt2[:, 48:56], lt2[:, 40:48], EPS
                                        ).then_inc(*S.inc("ln2"))
            nc.scalar.wait_ge(S["ln2"], S.cnt["ln2"])
            nc.scalar.activation(lt2[:, 40:48], lt2[:, 48:56], AF.Ln)
            nc.scalar.drain()
            nc.scalar.activation(lt2[:, 48:56], lt2[:, 40:48], AF.Exp,
                                 scale=-0.5).then_inc(*S.inc("ln2"))
            nc.vector.wait_ge(S["ln2"], S.cnt["ln2"])
            nc.vector.tensor_tensor(lt2[:, 56:64], lt2[:, 16:24],
                                    lt2[:, 48:56], ALU.mult
                                    ).then_inc(*S.inc("ln2"))
            nc.scalar.wait_ge(S["ln2"], S.cnt["ln2"])
            for qt in range(8):
                rseg = res2[:, qt * D:(qt + 1) * D]
                nc.scalar.activation(rseg, rseg, AF.Identity,
                                     bias=lt2[:, 56 + qt:57 + qt],
                                     scale=lt2[:, 48 + qt:49 + qt]
                                     ).then_inc(*S.inc("ln2"))
                nc.sync.wait_ge(S["ln2"], S.cnt["ln2"])
                nc.sync.dma_start(d_out[qt * 128:(qt + 1) * 128, :], rseg
                                  ).then_inc(*S.inc("outd", 16))
            nc.sync.wait_ge(S["outd"], S.cnt["outd"])

        f1_right.close()

    return nc, sorted(dbg_outs)


def _rope_tables(t0, n):
    inv = 10000.0 ** (-(np.arange(32, dtype=np.float64) * 2) / 64.0)
    p = np.arange(128)
    t = np.arange(t0, t0 + n, dtype=np.float64)
    ang = t[None, :] * inv[p % 32][:, None]
    cos = np.cos(ang).astype(np.float32)
    sin = np.sin(ang).astype(np.float32)
    sgn = np.where((p % 64) < 32, -1.0, 1.0).astype(np.float32)
    return cos, np.ascontiguousarray(sin * sgn[:, None]).astype(np.float32)


def _prep_inputs(inp):
    c = np.ascontiguousarray

    def _blk(wT, nk):
        a = wT.reshape(2, 8, 128, nk, 128)
        return c(a.transpose(3, 2, 0, 1, 4).reshape(nk * 128, 2048))

    x = np.asarray(inp["L0_attn_x"], dtype=np.float32)
    cosK, sinK = _rope_tables(0, T)
    base = {
        "wqT": c(np.asarray(inp["L0_attn_Wq"]).T),
        "wkT": c(np.asarray(inp["L0_attn_Wk"]).T),
        "wvT": c(np.asarray(inp["L0_attn_Wv"]).T),
        "woT": c(np.asarray(inp["L0_attn_Wo"]).T),
        "bqt": c(np.asarray(inp["L0_attn_bq"]).reshape(16, 128).T),
        "bkt": c(np.asarray(inp["L0_attn_bk"]).reshape(4, 128).T),
        "bvb": c(np.broadcast_to(np.asarray(inp["L0_attn_bv"])[None, :],
                                 (128, 512))),
        "bob": c(np.broadcast_to(np.asarray(inp["L0_attn_bo"])[None, :],
                                 (128, D))),
        "gwT": c(np.asarray(inp["L0_ffn_gate_w"]).T),
        "uwT": c(np.asarray(inp["L0_ffn_up_w"]).T),
        "dwT": c(np.asarray(inp["L0_ffn_down_w"]).T),
        "gst": c(np.asarray(inp["L0_ffn_gate_s"]).reshape(NIT, 128).T),
        "ust": c(np.asarray(inp["L0_ffn_up_s"]).reshape(NIT, 128).T),
        "dsb": c(np.broadcast_to(np.asarray(inp["L0_ffn_down_s"])[None, :],
                                 (128, D))),
        "cosK": cosK, "sinK": sinK,
        "identb": np.eye(128).astype(ml_dtypes.bfloat16),
    }
    xT = [c(x[b].T) for b in range(B)]
    maps = []
    for core in range(N_CORES):
        b = core // 4
        q0 = (core % 4) * TOK
        cosQ, sinQ = _rope_tables(q0, TOK)
        m = dict(base)
        m["xT"] = xT[b]
        m["xqT"] = c(xT[b][:, q0:q0 + TOK])
        m["xres"] = c(x[b, q0:q0 + TOK])
        m["cosQ"] = cosQ
        m["sinQ"] = sinQ
        maps.append(m)
    return maps


def kernel(**inputs):
    try:
        from concourse.compiler_utils import (get_compiler_flags,
                                              set_compiler_flags)
        set_compiler_flags([
            f.replace("--enable-ldw-opt=false", "--enable-ldw-opt=true")
            for f in get_compiler_flags()])
    except Exception:
        pass
    maps = _prep_inputs(inputs)
    nc, _ = build_program()
    res = run_bass_kernel_spmd(nc, maps, core_ids=list(range(N_CORES)))
    out = np.empty((B, T, D), np.float32)
    for core in range(N_CORES):
        b = core // 4
        q0 = (core % 4) * TOK
        out[b, q0:q0 + TOK] = np.asarray(res.results[core]["out"])
    return out
